# version 12
# speedup vs baseline: 1.1103x; 1.0035x over previous
import numpy as np
import ml_dtypes
from contextlib import ExitStack

import concourse.bass as bass
import concourse.mybir as mybir
from concourse.bass_utils import run_bass_kernel_spmd
from concourse.alu_op_type import AluOpType as OP

F32 = mybir.dt.float32
BF16 = mybir.dt.bfloat16
AF = mybir.ActivationFunctionType
AX = mybir.AxisListType

BF = ml_dtypes.bfloat16

D = 1024
TILE = 128
HALO_T = 9
OWN_T = 16
SPAN_T = HALO_T + OWN_T
PRE_T = (8192 - SPAN_T * TILE) // TILE
SPAN = SPAN_T * TILE
PRE = PRE_T * TILE
LNEPS = 1e-5

LA_H = 16
LA_D = 64
NH = 8
DH = 64
INNER = 512
WIN = 512
HID = 2730
H2 = 2 * HID
QK_SCALE = 8.0

B0A_KV = range(0, 4)
B0A_FULL = range(4, SPAN_T)
B0F = range(4, SPAN_T)
B1A_KV = range(4, 8)
B1A_FULL = range(8, SPAN_T)
B1F = range(HALO_T, SPAN_T)
CT = range(HALO_T, SPAN_T)

ENGS = ("sync", "tensor", "vector", "scalar", "gpsimd")


_STEP_SEQ = [0]


class Step:
    __slots__ = ("eng", "reads", "writes", "emits", "dma", "tok", "waits",
                 "needs_tok", "clock", "tag", "sid", "depw")

    def __init__(self, eng, reads, writes, emits, dma, tag):
        _STEP_SEQ[0] += 1
        self.sid = _STEP_SEQ[0]
        self.depw = None
        self.eng = eng
        self.reads = reads
        self.writes = writes
        self.emits = emits
        self.dma = dma
        self.tok = None
        self.waits = []
        self.needs_tok = False
        self.clock = None
        self.tag = tag


def _ov(a, b):
    return a[0] == b[0] and a[1] < b[3] and b[1] < a[3] if False else None


class Prog:

    def __init__(self, nc):
        self.nc = nc
        self.steps = []
        self.phases = []
        self._cur_phase = None
        self._cur_lo = 0
        self._events = {}

    def phase(self, name):
        if self._cur_phase is not None:
            self.phases.append((self._cur_phase, self._cur_lo, len(self.steps)))
        self._cur_phase = name
        self._cur_lo = len(self.steps)

    def step(self, eng, reads=(), writes=(), f=None, dma=False, tag=""):
        st = Step(eng, list(reads), list(writes), f, dma, tag)
        self.steps.append(st)
        return st

    def begin_capture(self):
        self._cap = len(self.steps)
        self._split = None

    def mark_split(self):
        self._split = len(self.steps)

    def capture(self):
        g = self.steps[self._cap:]
        sp = (self._split - self._cap) if self._split is not None else len(g)
        del self.steps[self._cap:]
        return g[:sp], g[sp:]

    def emit_interleaved(self, a, b, lag):
        self.steps.extend(a[:lag])
        ai, bi = lag, 0
        while ai < len(a) or bi < len(b):
            if ai < len(a):
                self.steps.append(a[ai]); ai += 1
            if bi < len(b):
                self.steps.append(b[bi]); bi += 1

    def finalize(self):
        if self._cur_phase is not None:
            self.phases.append((self._cur_phase, self._cur_lo, len(self.steps)))
            self._cur_phase = None
        deps_of = [None] * len(self.steps)
        for i, st in enumerate(self.steps):
            deps = set()
            depw = set()
            for (name, lo, hi) in st.reads:
                for (elo, ehi, es, wr) in self._events.get(name, ()):
                    if wr and elo < hi and lo < ehi:
                        deps.add(es)
                        depw.add((name, lo, hi, self.steps[es].sid))
            for (name, lo, hi) in st.writes:
                for (elo, ehi, es, wr) in self._events.get(name, ()):
                    if elo < hi and lo < ehi:
                        deps.add(es)
            st.depw = depw
            deps_of[i] = deps
            for (name, lo, hi) in st.reads:
                self._events.setdefault(name, []).append((lo, hi, i, False))
            for (name, lo, hi) in st.writes:
                ev = self._events.setdefault(name, [])
                ev[:] = [e for e in ev if not (lo <= e[0] and e[1] <= hi)]
                ev.append((lo, hi, i, True))
        RROT = 8
        for i, st in enumerate(self.steps):
            for d in deps_of[i]:
                self.steps[d].needs_tok = True
        counters = {}
        dma_rr = {}
        for st in self.steps:
            if st.dma:
                rr = dma_rr.get(st.eng, 0)
                dma_rr[st.eng] = rr + 1
                key = f"{st.eng}_dma_{rr % RROT}"
                counters[key] = counters.get(key, 0) + 1
                st.tok = (key, counters[key])
            elif st.needs_tok:
                key = st.eng
                counters[key] = counters.get(key, 0) + 1
                st.tok = (key, counters[key])
        self.sem_keys = sorted(counters)
        watermark = {e: {} for e in ENGS}
        for i, st in enumerate(self.steps):
            wm = watermark[st.eng]
            if st.dma and st.tok[1] > 1:
                key, cnt = st.tok
                if wm.get(key, 0) < cnt - 1:
                    st.waits.append((key, cnt - 1))
                    wm[key] = cnt - 1
            dl = sorted(deps_of[i], key=lambda d: -(self.steps[d].tok[1]))
            for d in dl:
                dst = self.steps[d]
                key, cnt = dst.tok
                if wm.get(key, 0) >= cnt:
                    continue
                st.waits.append((key, cnt))
                wm[key] = cnt
                for k2, c2 in dst.clock.items():
                    if wm.get(k2, 0) < c2:
                        wm[k2] = c2
            cl = dict(wm)
            if st.tok is not None:
                cl[st.tok[0]] = st.tok[1]
            st.clock = cl
        return counters

    def emit(self, sem_handles):
        nc = self.nc
        for (pname, lo, hi) in self.phases:
            chunk = self.steps[lo:hi]
            per_eng = {e: [s for s in chunk if s.eng == e] for e in ENGS}
            with nc.Block(pname.replace("/", "_")) as blk:
                for ename in ENGS:
                    sts = per_eng[ename]
                    if not sts:
                        continue

                    def make(sts=sts):
                        def body(eng):
                            for st in sts:
                                for (key, cnt) in st.waits:
                                    v = cnt * 16 if "_dma_" in key else cnt
                                    eng.wait_ge(sem_handles[key], v)
                                inst = st.emits(eng)
                                if st.tok is not None:
                                    n = 16 if st.dma else 1
                                    if inst is None:
                                        inst = eng.nop()
                                    inst.then_inc(sem_handles[st.tok[0]], n)
                        return body

                    getattr(blk, ename)(make())


def cdiv(a, b):
    return (a + b - 1) // b


class KB:

    def __init__(self, es, dbg=False, a_limit=None, a_parts=63, il_lag=4):
        nc = bass.Bass("TRN2", target_bir_lowering=False)
        self.nc = nc
        self.es = es
        self.dbg = dbg
        self.a_limit = a_limit
        self.a_parts = a_parts
        self.il_lag = il_lag
        self.P = Prog(nc)
        self._mk_io()
        self._mk_sbuf()
        self._mk_psum()

    def r_cols(self, obj, lo, hi, itemsize):
        off = self._views.get(id(obj)) if hasattr(self, "_views") else None
        if off is not None:
            return ("arena", (off * 2) + lo * itemsize, (off * 2) + hi * itemsize)
        return (obj.name, lo * itemsize, hi * itemsize)

    def r_rows(self, handle, lo, hi, rowbytes):
        return (handle.name, lo * rowbytes, hi * rowbytes)

    def _mk_io(self):
        nc = self.nc
        di = lambda n, s, dt=F32: nc.dram_tensor(n, s, dt, kind="ExternalInput")
        self.i_x0 = di("x0stage", [8192, D])
        self.i_rmask = di("rmask", [8192, 1])
        self.i_m01 = di("m01", [SPAN_T, TILE, 640], BF16)
        self.i_rot = di("rot", [SPAN, 256])
        self.i_scl = di("scl", [2, TILE, 256])
        self.i_lnrep = di("lnrep", [2, TILE, 2048])
        self.i_cmask = di("cmaskI", [TILE, TILE], BF16)
        self.i_ident = di("identI", [TILE, TILE], BF16)
        self.i_epsc = di("epsc", [TILE, 8])
        self.i_qkvw = di("qkvwT", [D, 3 * D])
        self.i_qkvb = di("qkvb", [1, 3 * D])
        self.i_outw = di("outwT", [D, D])
        self.i_outb = di("outb", [1, D])
        self.i_wqkv = di("wqkv", [2, D, 3 * INNER])
        self.i_wo = di("wo", [2, INNER, D])
        self.i_w1 = di("w1", [2, D, H2])
        self.i_b1 = di("b1", [2, 1, H2])
        self.i_w2 = di("w2", [2, HID, D])
        self.i_b2 = di("b2", [2, 1, D])
        self.o_y = nc.dram_tensor("y", [OWN_T * TILE, D], F32, kind="ExternalOutput")
        dd = lambda n, s, dt=F32: nc.dram_tensor(n, s, dt, kind="Internal")
        self.d_x1 = dd("x1buf", [SPAN, D])
        self.d_sa = dd("sbufA", [SPAN, D])
        self.d_sb = dd("sbufB", [SPAN, D])
        self.d_hln = dd("hlnbuf", [SPAN, D], BF16)
        if self.dbg:
            self.d_dbg = dd("dbgbuf", [SPAN, 4 * 1024 + 1040], BF16)
            self.d_dbg2 = dd("dbgbuf2", [TILE, 8192], BF16)

    def _mk_sbuf(self):
        nc, es = self.nc, self.es
        sb = lambda n, s, dt: es.enter_context(nc.sbuf_tensor(n, s, dt))
        self.tin = [sb("tin0", [TILE, D], F32), sb("tin1", [TILE, D], F32)]
        self.taux = [sb("taux0", [TILE, D], F32), sb("taux1", [TILE, D], F32)]
        self.qc = [sb("qc0", [TILE, 512], F32), sb("qc1", [TILE, 512], F32),
                   sb("qc2", [TILE, 512], F32), sb("qc3", [TILE, 512], F32),
                   sb("qc4", [TILE, 512], F32), sb("qc5", [TILE, 512], F32)]
        self.gel = sb("gelf", [TILE, 512], F32)
        self.gel2 = sb("gelf2", [TILE, 512], F32)
        self.rt = [sb("rt0", [TILE, 256], F32), sb("rt1", [TILE, 256], F32)]
        self.rprep = sb("rprep", [TILE, 256], F32)
        self.rprep2 = sb("rprep2", [TILE, 256], F32)
        self.scl = sb("sclb", [TILE, 256], F32)
        self.stt = sb("stt", [TILE, 96], F32)
        self.strst = sb("strst", [TILE, SPAN_T], F32)
        self.stnmr = sb("stnmr", [TILE, SPAN_T], F32)
        self.epsc = sb("epscb", [TILE, 8], F32)
        self.mrow = [sb("mrow0", [TILE, 1], F32), sb("mrow1", [TILE, 1], F32)]
        self.tbf = [sb("tbf0", [TILE, D], BF16), sb("tbf1", [TILE, D], BF16)]
        self.cmask = sb("cmaskb", [TILE, TILE], BF16)
        self.ident = sb("identb", [TILE, TILE], BF16)
        self.ones_row = sb("ones_row", [1, TILE], BF16)
        self.zrow = sb("zrow", [1, 512], BF16)
        self.A_W1 = 0
        self.A_W2 = 43680
        self.A_B1R = 66208
        self.A_B2R = 71668
        self.A_WQKV = 0
        self.A_WO = 12288
        self.A_KFM = 16384
        self.A_VSP = 22528
        self.A_XFM = 25648
        self.A_LNG = 77300
        self.A_QFMB = 27712
        self.A_QROTB = 28736
        self.A_KROTB = 29248
        self.A_OTMB = 29760
        self.A_OFMB = 30272
        self.A_PBF = 30784
        self.A_PTSB = 32064
        self.A_M01 = 33344
        self.A_QFMB2 = 34624
        self.A_PBF2 = 35136
        self.A_PTSB2 = 36416
        self.A_QKVW = 0
        self.A_OUTW = 24576
        self.A_QKVBR = 32768
        self.A_OUTBR = 35840
        self.A_QTM = 36864
        self.A_KTM = 37888
        self.A_QFMA = 38912
        self.A_KFMA = 40960
        self.A_OTM = 43008
        self.A_OFM = 44032
        self.A_VTA = 45056
        self.A_AMSB = 46096
        self.A_SSB = 47120
        self.A_ZSB = 48144
        self.A_VTA2 = 48160
        self.A_QFMA2 = 49200
        self.A_KFMA2 = 51248
        self.A_OTM2 = 53296
        self.A_QROTB2 = 35136
        self.A_KROTB2 = 35648
        self.A_OTMB2 = 36160
        self.A_OFMB2 = 36672
        self.A_GLBF = 72692
        self.A_H1FM = 73204
        self.A_LNJ = 75252
        self.A_GFM = 81396
        self.A_HSX = 66208
        self.AR_COLS = 81908
        self.arena = sb("arena", [TILE, self.AR_COLS], BF16)
        self._views = {}

        def view(off, n, parts=TILE):
            ap = self.arena[0:parts, off:off + n]
            self._views[id(ap)] = off
            return ap

        self.qtm = view(self.A_QTM, D)
        self.ktm = view(self.A_KTM, D)
        self.qfmA_ = [view(self.A_QFMA, 2048, parts=64),
                      view(self.A_QFMA2, 2048, parts=64)]
        self.kfmA_ = [view(self.A_KFMA, 2048, parts=64),
                      view(self.A_KFMA2, 2048, parts=64)]
        self.vtA_ = [view(self.A_VTA, LA_H * 65), view(self.A_VTA2, LA_H * 65)]
        self.otm_ = [view(self.A_OTM, D), view(self.A_OTM2, D)]
        self.ofm = view(self.A_OFM, D)
        self.amsb = [view(self.A_AMSB, 512), view(self.A_AMSB + 512, 512)]
        self.ssb = view(self.A_SSB, 1024, parts=64)
        self.zsb = view(self.A_ZSB, 16, parts=64)
        self.pbf = [view(self.A_PBF, 640), view(self.A_PBF + 640, 640),
                    view(self.A_PBF2, 640), view(self.A_PBF2 + 640, 640)]
        self.ptsb = [view(self.A_PTSB, 640), view(self.A_PTSB + 640, 640),
                     view(self.A_PTSB2, 640), view(self.A_PTSB2 + 640, 640)]
        self.m01 = [view(self.A_M01, 640), view(self.A_M01 + 640, 640)]
        self.qfmB_ = [view(self.A_QFMB, 1024, parts=64),
                      view(self.A_QFMB2, 1024, parts=64)]
        self.qrotB_ = [view(self.A_QROTB, INNER), view(self.A_QROTB2, INNER)]
        self.krotB_ = [view(self.A_KROTB, INNER), view(self.A_KROTB2, INNER)]
        self.otmB_ = [view(self.A_OTMB, INNER), view(self.A_OTMB2, INNER)]
        self.ofmB_ = [view(self.A_OFMB, INNER), view(self.A_OFMB2, INNER)]
        self.glbf = view(self.A_GLBF, 512)
        self.hbrow = sb("hbrow", [1, 1024], F32)
        self.h1fm = [view(self.A_H1FM, D), view(self.A_H1FM + D, D)]
        self.lnjunk = view(self.A_LNJ, 2048)
        self.gfmB_ = [view(self.A_GFM, 512), view(self.A_LNJ + 512, 512)]
        self.glbf2 = view(self.A_LNJ, 512)
        self.hsx = [self.ar_f32(self.A_HSX, 1024), self.ar_f32(self.A_HSX + 2048, 1024)]
        self.qkvb_bf = view(self.A_QKVBR, 3 * D, parts=1)
        self.outb_bf = view(self.A_OUTBR, D, parts=1)
        self.b1_bf = view(self.A_B1R, H2, parts=1)
        self.b2_bf = view(self.A_B2R, D, parts=1)

    def _mk_psum(self):
        nc, es = self.nc, self.es
        ps = lambda n, s, dt: es.enter_context(nc.psum_tensor(n, s, dt))
        self.pb0 = ps("pb0", [TILE, 512], F32)
        self.pb1 = ps("pb1", [TILE, 512], F32)
        self.psA = ps("psA", [TILE, 1024], F32)
        self.psB = ps("psB", [TILE, 1024], F32)
        self.pt0 = ps("pt0", [TILE, 1024], BF16)
        self.pt1 = ps("pt1", [TILE, 1024], BF16)

    def r_bank(self, handle, bank):
        bb = 2048
        return (handle.name, bank * bb, (bank + 1) * bb)

    def r_all(self, obj):
        off = self._views.get(id(obj)) if hasattr(self, "_views") else None
        if off is not None:
            n = obj.shape[-1]
            return ("arena", off * 2, (off + n) * 2)
        return (obj.name, 0, 1 << 40)

    def ar(self, lo, n):
        return self.arena[:, lo:lo + n]

    def r_ar(self, lo, n):
        return ("arena", lo * 2, (lo + n) * 2)

    def ar_f32(self, lo_bf, n_f32):
        v = self.arena[:, :].bitcast(F32)
        return v[:, lo_bf // 2: lo_bf // 2 + n_f32]

    def build(self, stop_after=None):
        todo = [self.ph_init, self.ph_A,
                lambda: self.ph_Ba(0), lambda: self.ph_Bf(0),
                lambda: self.ph_Ba(1), lambda: self.ph_Bf(1), self.ph_C]
        names = ["init", "A", "B0a", "B0f", "B1a", "B1f", "C"]
        for nm, fn in zip(names, todo):
            fn()
            if stop_after is not None and nm == stop_after:
                break
        counters = self.P.finalize()
        sems = {}
        for k in sorted(counters):
            sems[k] = self.es.enter_context(self.nc.semaphore("s_" + k))
        self.P.emit(sems)
        return self.nc

    def ph_init(self):
        P, K = self.P, self
        P.phase("init")
        it = F32
        P.step("sync", writes=[K.r_all(K.cmask)],
               f=lambda e: e.dma_start(out=K.cmask[:, :], in_=K.i_cmask.ap()), dma=True)
        P.step("sync", writes=[K.r_all(K.ident)],
               f=lambda e: e.dma_start(out=K.ident[:, :], in_=K.i_ident.ap()), dma=True)
        P.step("sync", writes=[K.r_all(K.epsc)],
               f=lambda e: e.dma_start(out=K.epsc[:, :], in_=K.i_epsc.ap()), dma=True)
        P.step("vector", writes=[K.r_all(K.ones_row)],
               f=lambda e: e.memset(K.ones_row[:, :], 1.0))
        P.step("vector", writes=[K.r_all(K.zrow)],
               f=lambda e: e.memset(K.zrow[:, :], 0.0))
        P.step("vector", writes=[K.r_all(K.tin[0])],
               f=lambda e: e.memset(K.tin[0][:, :], 0.0))
        rowb = D * 4
        for buf in (K.d_x1, K.d_sa, K.d_sb):
            for t in range(10):
                P.step("gpsimd", reads=[K.r_all(K.tin[0])],
                       writes=[K.r_rows(buf, t * TILE, (t + 1) * TILE, rowb)],
                       f=lambda e, buf=buf, t=t: e.dma_start(
                           out=buf.ap()[t * TILE:(t + 1) * TILE, :], in_=K.tin[0][:, :]),
                       dma=True)
        P.step("gpsimd", writes=[K.r_ar(K.A_QKVW, 8 * 3072)],
               f=lambda e: e.dma_start(
                   out=K.ar(K.A_QKVW, 8 * 3072).rearrange("p (c n) -> p c n", c=8),
                   in_=K.i_qkvw.ap().rearrange("(c p) n -> p c n", p=TILE)), dma=True)
        P.step("gpsimd", writes=[K.r_ar(K.A_OUTW, 8 * 1024)],
               f=lambda e: e.dma_start(
                   out=K.ar(K.A_OUTW, 8 * 1024).rearrange("p (c n) -> p c n", c=8),
                   in_=K.i_outw.ap().rearrange("(c p) n -> p c n", p=TILE)), dma=True)
        P.step("gpsimd", writes=[K.r_all(K.qkvb_bf)],
               f=lambda e: e.dma_start(out=K.qkvb_bf[:, :], in_=K.i_qkvb.ap()), dma=True)
        P.step("gpsimd", writes=[K.r_all(K.outb_bf)],
               f=lambda e: e.dma_start(out=K.outb_bf[:, :], in_=K.i_outb.ap()), dma=True)

    def _ln_stats(self, src_reads, src_ap, n, rstd_col, nmr_col, mask_mul=None,
                  par=0):
        P, K = self.P, self
        stt = K.stt
        c_s1, c_ss, c_mu, c_var = 32 + 4 * par, 33 + 4 * par, 34 + 4 * par, 35 + 4 * par
        r_out = [K.r_cols(stt, rstd_col, rstd_col + 1, 4),
                 K.r_cols(stt, nmr_col, nmr_col + 1, 4)]
        r_tmp = [K.r_cols(stt, c_s1, c_s1 + 4, 4)]
        P.step("vector", reads=src_reads, writes=[r_tmp[0]],
               f=lambda e: e.reduce_sum(stt[:, c_s1:c_s1 + 1], src_ap, axis=AX.X))
        jo = (par % 2) * 1024
        P.step("scalar", reads=src_reads,
               writes=[r_tmp[0], K.r_cols(K.lnjunk, jo, jo + n, 2)],
               f=lambda e: e.activation(K.lnjunk[:, jo:jo + n],
                                        src_ap, AF.Square,
                                        accum_out=stt[:, c_ss:c_ss + 1]))
        def t1(e):
            e.tensor_scalar(stt[:, c_mu:c_mu + 1], stt[:, c_s1:c_s1 + 1],
                            1.0 / n, None, OP.mult)
            return e.tensor_scalar(stt[:, c_var:c_var + 1], stt[:, c_ss:c_ss + 1],
                                   1.0 / n, None, OP.mult)
        P.step("vector", reads=[r_tmp[0]], writes=[r_tmp[0]], f=t1)
        P.step("vector", reads=[r_tmp[0]], writes=[r_tmp[0]],
               f=lambda e: e.tensor_tensor(stt[:, c_s1:c_s1 + 1],
                                           stt[:, c_mu:c_mu + 1],
                                           stt[:, c_mu:c_mu + 1], OP.mult))
        P.step("vector", reads=[r_tmp[0]], writes=[r_tmp[0]],
               f=lambda e: e.tensor_tensor(stt[:, c_var:c_var + 1],
                                           stt[:, c_var:c_var + 1],
                                           stt[:, c_s1:c_s1 + 1], OP.subtract))
        P.step("scalar", reads=[r_tmp[0], K.r_all(K.epsc)], writes=[r_tmp[0]],
               f=lambda e: e.activation(stt[:, c_var:c_var + 1],
                                        stt[:, c_var:c_var + 1], AF.Ln,
                                        bias=K.epsc[:, 0:1]))
        P.step("scalar", reads=[r_tmp[0]], writes=[r_tmp[0], r_out[0]],
               f=lambda e: e.activation(stt[:, rstd_col:rstd_col + 1],
                                        stt[:, c_var:c_var + 1], AF.Exp, scale=-0.5))
        if mask_mul is not None:
            P.step("vector", reads=[r_out[0], mask_mul[1]], writes=[r_out[0]],
                   f=lambda e: e.tensor_tensor(
                       stt[:, rstd_col:rstd_col + 1], stt[:, rstd_col:rstd_col + 1],
                       mask_mul[0], OP.mult))
        P.step("vector", reads=[r_tmp[0], r_out[0]], writes=[r_out[1]],
               f=lambda e: e.scalar_tensor_tensor(
                   out=stt[:, nmr_col:nmr_col + 1], in0=stt[:, c_mu:c_mu + 1],
                   scalar=-1.0, in1=stt[:, rstd_col:rstd_col + 1],
                   op0=OP.mult, op1=OP.mult))
        return r_out

    def _transpose8(self, src, src_reads, pt, dst_ap, dst_writes, nch=8):
        P, K = self.P, self
        def tr(e):
            last = None
            for c in range(nch):
                last = e.transpose(pt[:, c * TILE:(c + 1) * TILE],
                                   src[:, c * TILE:(c + 1) * TILE], K.ident[:, :])
            return last
        P.step("tensor", reads=src_reads + [K.r_all(K.ident)],
               writes=[K.r_all(pt)], f=tr)
        P.step("vector", reads=[K.r_all(pt)], writes=dst_writes,
               f=lambda e: e.tensor_copy(dst_ap, pt[:, 0:nch * TILE]))

    def ph_A(self):
        P, K = self.P, self
        P.phase("A")
        rowb = D * 4
        qkvw = lambda c, lo, n: K.ar(K.A_QKVW + c * 3072 + lo, n)
        outw = lambda c, lo, n: K.ar(K.A_OUTW + c * 1024 + lo, n)
        r_qkvw = K.r_ar(K.A_QKVW, 8 * 3072)
        r_outw = K.r_ar(K.A_OUTW, 8 * 1024)
        for sbank in range(2):
            P.step("tensor", reads=[K.r_all(K.zrow)], writes=[K.r_bank(K.psA, sbank)],
                   f=lambda e, sbank=sbank: e.matmul(
                       K.psA[:, sbank * 512:(sbank + 1) * 512],
                       K.zrow[0:1, 0:TILE], K.zrow[0:1, 0:512],
                       start=True, stop=False, skip_group_check=True))
        P.step("tensor", reads=[K.r_all(K.zrow)], writes=[K.r_all(K.pb1)],
               f=lambda e: e.matmul(K.pb1[:, :], K.zrow[0:1, 0:TILE],
                                    K.zrow[0:1, 0:512],
                                    start=True, stop=False, skip_group_check=True))

        ALL_T = PRE_T + SPAN_T
        if self.a_limit is not None:
            ALL_T = min(ALL_T, self.a_limit)
        for t in range(ALL_T):
            span = t >= PRE_T
            tt = t - PRE_T
            buf = t % 2
            xin = K.tin[buf]
            r_xin = K.r_all(xin)
            row0 = t * TILE
            P.step("sync", writes=[r_xin], dma=True,
                   f=lambda e, row0=row0, xin=xin: e.dma_start(
                       out=xin[:, :], in_=K.i_x0.ap()[row0:row0 + TILE, :]))
            mr = K.mrow[buf]
            r_mr = K.r_all(mr)
            P.step("sync", writes=[r_mr], dma=True,
                   f=lambda e, row0=row0, mr=mr: e.dma_start(
                       out=mr[:, :], in_=K.i_rmask.ap()[row0:row0 + TILE, :]))
            rc = buf * 2
            self._ln_stats([r_xin], xin[:, :], D, rc, rc + 1, par=buf)
            h1 = K.tbf[buf]
            r_h1 = K.r_all(h1)
            P.step("scalar", reads=[r_xin, K.r_cols(K.stt, rc, rc + 2, 4)],
                   writes=[r_h1],
                   f=lambda e, xin=xin, h1=h1, rc=rc: e.activation(
                       h1[:, :], xin[:, :], AF.Identity,
                       scale=K.stt[:, rc:rc + 1], bias=K.stt[:, rc + 1:rc + 2]))
            hfm = K.h1fm[buf]
            r_hfm = K.r_all(hfm)
            self._transpose8(h1, [r_h1], K.pt0, hfm[:, :], [r_hfm])
            g0 = 0 if span else 2
            for g in range(g0, 6):
                pb = (K.pb0[:, :], K.psB[:, 0:512])[g % 2]
                r_pb = K.r_all(K.pb0) if g % 2 == 0 else K.r_bank(K.psB, 0)
                def mm(e, g=g, hfm=hfm, pb=pb):
                    for c in range(8):
                        e.matmul(pb, hfm[:, c * TILE:(c + 1) * TILE],
                                 qkvw(c, g * 512, 512), start=(c == 0), stop=False)
                    return e.matmul(pb, K.ones_row[:, :],
                                    K.qkvb_bf[:, g * 512:(g + 1) * 512],
                                    start=False, stop=True)
                P.step("tensor", reads=[r_hfm, r_qkvw, K.r_all(K.qkvb_bf),
                                        K.r_all(K.ones_row)],
                       writes=[r_pb], f=mm)
                if g < 2:
                    P.step("vector", reads=[r_pb], writes=[K.r_cols(K.qtm, g * 512, g * 512 + 512, 2)],
                           f=lambda e, g=g, pb=pb: e.tensor_scalar(
                               K.qtm[:, g * 512:(g + 1) * 512], pb,
                               0.0, 1e-6, OP.max, OP.add))
                elif g < 4:
                    gk = g - 2
                    r_qc = K.r_all(K.qc[gk])
                    P.step("vector", reads=[r_pb], writes=[r_qc],
                           f=lambda e, gk=gk, pb=pb: e.tensor_scalar(
                               K.qc[gk][:, :], pb, 0.0, 1e-6, OP.max, OP.add))
                    P.step("vector", reads=[r_qc, r_mr],
                           writes=[K.r_cols(K.ktm, gk * 512, gk * 512 + 512, 2)],
                           f=lambda e, gk=gk, mr=mr: e.tensor_tensor(
                               K.ktm[:, gk * 512:(gk + 1) * 512], K.qc[gk][:, :],
                               mr[:, 0:1].broadcast_to([TILE, 512]), OP.mult))
                else:
                    gv = g - 4
                    vtA = K.vtA_[buf]
                    P.step("vector", reads=[r_pb],
                           writes=[K.r_cols(vtA, gv * 8 * 65, (gv + 1) * 8 * 65, 2)],
                           f=lambda e, gv=gv, pb=pb, vtA=vtA: e.tensor_copy(
                               vtA[:, :].rearrange("p (h c) -> p h c", c=65)
                               [:, gv * 8:(gv + 1) * 8, 0:64],
                               pb.rearrange("p (h c) -> p h c", c=64)))
            if t in (0, 1):
                vtA = K.vtA_[buf]
                P.step("vector", writes=[K.r_all(vtA)], tag="vones",
                       f=lambda e, vtA=vtA: e.memset(
                           vtA[:, :].rearrange("p (h c) -> p h c", c=65)[:, :, 64:65], 1.0))

            r_ssb = K.r_all(K.ssb)
            if span and not (K.a_parts & 1):
                span = False
            if span:
                r_zsb = K.r_all(K.zsb)
                P.step("vector", reads=[K.r_bank(K.psA, 0), K.r_bank(K.psA, 1)],
                       writes=[r_ssb],
                       f=lambda e: e.tensor_copy(K.ssb[0:64, :], K.psA[0:64, :]))
                P.step("vector", reads=[K.r_all(K.pb1)], writes=[r_zsb],
                       f=lambda e: e.tensor_copy(K.zsb[0:64, :], K.pb1[0:64, 0:16]))
                kfmA, qfmA = K.kfmA_[buf], K.qfmA_[buf]
                otm = K.otm_[buf]
                for src_tm, dst_fm, r_dst in ((K.ktm, kfmA, K.r_all(kfmA)),
                                              (K.qtm, qfmA, K.r_all(qfmA))):
                    for half in range(2):
                        def trh(e, src_tm=src_tm, half=half):
                            last = None
                            for j in range(8):
                                h = half * 8 + j
                                last = e.transpose(
                                    K.pt0[0:64, j * TILE:(j + 1) * TILE],
                                    src_tm[:, h * 64:(h + 1) * 64], K.ident[:, :])
                            return last
                        P.step("tensor", reads=[K.r_all(src_tm), K.r_all(K.ident)],
                               writes=[K.r_all(K.pt0)], f=trh)
                        P.step("vector", reads=[K.r_all(K.pt0)], writes=[r_dst],
                               f=lambda e, dst_fm=dst_fm, half=half: e.tensor_copy(
                                   dst_fm[0:64, half * 1024:(half + 1) * 1024],
                                   K.pt0[0:64, 0:1024]))
                for grp in (range(4) if (K.a_parts & 4) else []):
                    r_pbank = K.r_bank(K.psB, 1)
                    def at(e, grp=grp, kfmA=kfmA, qfmA=qfmA):
                        last = None
                        for j in range(4):
                            h = grp * 4 + j
                            last = e.matmul(
                                K.psB[:, 512 + j * TILE: 512 + (j + 1) * TILE],
                                kfmA[0:64, h * TILE:(h + 1) * TILE],
                                qfmA[0:64, h * TILE:(h + 1) * TILE],
                                start=True, stop=True)
                        return last
                    P.step("tensor", reads=[K.r_all(qfmA), K.r_all(kfmA)],
                           writes=[r_pbank], f=at)
                    am = K.amsb[grp % 2]
                    P.step("vector", reads=[r_pbank, K.r_all(K.cmask)],
                           writes=[K.r_all(am)],
                           f=lambda e, am=am: e.tensor_tensor(
                               am[:, :].rearrange("p (a b) -> p a b", a=4),
                               K.psB[:, 512:1024]
                               .rearrange("p (a b) -> p a b", a=4),
                               K.cmask[:, :].unsqueeze(1).broadcast_to([TILE, 4, TILE]),
                               OP.mult))
                    if not (K.a_parts & 8):
                        continue
                    nd = (K.pb0[:, :], K.psB[:, 0:512])[grp % 2]
                    r_nd = K.r_all(K.pb0) if grp % 2 == 0 else K.r_bank(K.psB, 0)
                    oc0 = 8 + 8 * (grp % 2)
                    def ndm(e, grp=grp, am=am, nd=nd, qfmA=qfmA, vtA=K.vtA_[buf]):
                        last = None
                        for j in range(4):
                            h = grp * 4 + j
                            e.matmul(nd[:, j * 65:(j + 1) * 65],
                                     am[:, j * TILE:(j + 1) * TILE],
                                     vtA[:, h * 65:(h + 1) * 65],
                                     start=True, stop=False)
                            e.matmul(nd[:, j * 65:j * 65 + 64],
                                     qfmA[0:64, h * TILE:(h + 1) * TILE],
                                     K.ssb[0:64, h * 64:(h + 1) * 64],
                                     start=False, stop=False)
                            last = e.matmul(nd[:, j * 65 + 64:j * 65 + 65],
                                            qfmA[0:64, h * TILE:(h + 1) * TILE],
                                            K.zsb[0:64, h:h + 1],
                                            start=False, stop=True)
                        return last
                    P.step("tensor", reads=[K.r_all(am), K.r_all(K.vtA_[buf]),
                                            K.r_all(qfmA), r_ssb, r_zsb],
                           writes=[r_nd], f=ndm)
                    P.step("vector", reads=[r_nd],
                           writes=[K.r_cols(K.stt, oc0, oc0 + 4, 4)],
                           f=lambda e, nd=nd, oc0=oc0: e.tensor_scalar(
                               K.stt[:, oc0:oc0 + 4],
                               nd[:, 0:260].rearrange("p (h c) -> p h c", c=65)
                               [:, :, 64:65].squeeze(2),
                               1e-6, None, OP.max))
                    P.step("vector", reads=[K.r_cols(K.stt, oc0, oc0 + 4, 4)],
                           writes=[K.r_cols(K.stt, oc0 + 4, oc0 + 8, 4)],
                           f=lambda e, oc0=oc0: e.reciprocal(
                               K.stt[:, oc0 + 4:oc0 + 8], K.stt[:, oc0:oc0 + 4]))
                    P.step("vector", reads=[r_nd, K.r_cols(K.stt, oc0 + 4, oc0 + 8, 4)],
                           writes=[K.r_cols(otm, grp * 256, (grp + 1) * 256, 2)],
                           f=lambda e, grp=grp, nd=nd, oc0=oc0, otm=otm: e.tensor_tensor(
                               otm[:, grp * 256:(grp + 1) * 256]
                               .rearrange("p (h c) -> p h c", c=64),
                               nd[:, 0:260].rearrange("p (h c) -> p h c", c=65)[:, :, 0:64],
                               K.stt[:, oc0 + 4:oc0 + 8].unsqueeze(2)
                               .broadcast_to([TILE, 4, 64]),
                               OP.mult))
            if K.a_parts & 16:
                def su(e, vtA=K.vtA_[buf]):
                    last = None
                    for h in range(LA_H):
                        e.matmul(K.psA[0:64, h * 64:(h + 1) * 64],
                                 K.ktm[:, h * 64:(h + 1) * 64],
                                 vtA[:, h * 65:h * 65 + 64],
                                 start=False, stop=False, skip_group_check=True)
                        last = e.matmul(K.pb1[0:64, h:h + 1],
                                        K.ktm[:, h * 64:(h + 1) * 64],
                                        vtA[:, h * 65 + 64:h * 65 + 65],
                                        start=False, stop=False, skip_group_check=True)
                    return last
                P.step("tensor", reads=[K.r_all(K.ktm), K.r_all(K.vtA_[buf])],
                       writes=[K.r_bank(K.psA, 0), K.r_bank(K.psA, 1),
                               K.r_all(K.pb1)], f=su)
            if span and not (K.a_parts & 32):
                span = False
            if span:
                self._transpose8(otm, [K.r_cols(otm, 0, 1024, 2)], K.pt0,
                                 K.ofm[:, :], [K.r_all(K.ofm)])
                for g in range(2):
                    bank = g
                    def om(e, g=g):
                        for c in range(8):
                            e.matmul(K.psB[:, g * 512:(g + 1) * 512],
                                     K.ofm[:, c * TILE:(c + 1) * TILE],
                                     outw(c, g * 512, 512), start=(c == 0), stop=False)
                        return e.matmul(K.psB[:, g * 512:(g + 1) * 512],
                                        K.ones_row[:, :],
                                        K.outb_bf[:, g * 512:(g + 1) * 512],
                                        start=False, stop=True)
                    P.step("tensor", reads=[K.r_all(K.ofm), r_outw,
                                            K.r_all(K.outb_bf), K.r_all(K.ones_row)],
                           writes=[K.r_bank(K.psB, g)], f=om)
                if K.dbg:
                    P.step("gpsimd", reads=[K.r_all(K.otm_[buf])], dma=True,
                           writes=[("dbgbuf", 1, 2)],
                           f=lambda e, tt=tt, otm=K.otm_[buf]: e.dma_start(
                               out=K.d_dbg.ap()[tt * TILE:(tt + 1) * TILE,
                                                3 * D + 1040:4 * D + 1040],
                               in_=otm[:, :]))
                aout = K.taux[0]
                r_aout = K.r_all(aout)
                P.step("scalar", reads=[K.r_bank(K.psB, 0), K.r_bank(K.psB, 1)],
                       writes=[r_aout],
                       f=lambda e, aout=aout: e.copy(aout[:, :], K.psB[:, :]))
                ra = 4 + 2 * buf
                self._ln_stats([r_aout], aout[:, :], D, ra, ra + 1,
                               mask_mul=(mr[:, 0:1], r_mr), par=buf)
                lna = K.taux[1]
                r_lna = K.r_all(lna)
                P.step("scalar", reads=[r_aout, K.r_cols(K.stt, ra, ra + 2, 4)],
                       writes=[r_lna],
                       f=lambda e, aout=aout, lna=lna, ra=ra: e.activation(
                           lna[:, :], aout[:, :], AF.Identity,
                           scale=K.stt[:, ra:ra + 1], bias=K.stt[:, ra + 1:ra + 2]))
                x1t = xin
                P.step("vector", reads=[r_lna, r_xin], writes=[r_xin],
                       f=lambda e, x1t=x1t, lna=lna: e.tensor_tensor(
                           x1t[:, :], x1t[:, :], lna[:, :], OP.add))
                P.step("gpsimd", reads=[r_xin], dma=True,
                       writes=[K.r_rows(K.d_x1, tt * TILE, (tt + 1) * TILE, rowb)],
                       f=lambda e, tt=tt, x1t=x1t: e.dma_start(
                           out=K.d_x1.ap()[tt * TILE:(tt + 1) * TILE, :], in_=x1t[:, :]))
        for sbank in range(2):
            P.step("tensor", reads=[K.r_all(K.zrow)], writes=[K.r_bank(K.psA, sbank)],
                   f=lambda e, sbank=sbank: e.matmul(
                       K.psA[:, sbank * 512:(sbank + 1) * 512],
                       K.zrow[0:1, 0:TILE], K.zrow[0:1, 0:512],
                       start=False, stop=True, skip_group_check=True))
        P.step("tensor", reads=[K.r_all(K.zrow)], writes=[K.r_all(K.pb1)],
               f=lambda e: e.matmul(K.pb1[:, :], K.zrow[0:1, 0:TILE],
                                    K.zrow[0:1, 0:512],
                                    start=False, stop=True, skip_group_check=True))

    def ph_Ba(self, l):
        P, K = self.P, self
        P.phase(f"B{l}a")
        rowb = D * 4
        src = K.d_x1 if l == 0 else K.d_sb
        dst = K.d_sa
        kv_tiles = B0A_KV if l == 0 else B1A_KV
        full_tiles = B0A_FULL if l == 0 else B1A_FULL
        tiles = list(kv_tiles) + list(full_tiles)
        r_wqkv = K.r_ar(K.A_WQKV, 8 * 1536)
        r_wo = K.r_ar(K.A_WO, 4 * 1024)
        wqkv = lambda c, lo, n: K.ar(K.A_WQKV + c * 1536 + lo, n)
        wo_w = lambda c, lo, n: K.ar(K.A_WO + c * 1024 + lo, n)
        P.step("gpsimd", writes=[r_wqkv], dma=True,
               f=lambda e: e.dma_start(
                   out=K.ar(K.A_WQKV, 8 * 1536).rearrange("p (c n) -> p c n", c=8),
                   in_=K.i_wqkv.ap()[l].rearrange("(c p) n -> p c n", p=TILE)))
        P.step("gpsimd", writes=[K.r_ar(K.A_W2, 22 * 1024)], dma=True,
               f=lambda e: e.dma_start(
                   out=K.ar(K.A_W2, 21 * 1024).rearrange("p (c n) -> p c n", c=21),
                   in_=K.i_w2.ap()[l][0:2688].rearrange("(c p) n -> p c n", p=TILE)))
        P.step("gpsimd", writes=[K.r_ar(K.A_W2, 22 * 1024)], dma=True,
               f=lambda e: e.dma_start(
                   out=K.ar(K.A_W2 + 21 * 1024, 1024)[0:42, :],
                   in_=K.i_w2.ap()[l][2688:2730]))
        P.step("gpsimd", writes=[K.r_ar(K.A_W1 + 7 * H2, H2)], dma=True,
               f=lambda e: e.dma_start(
                   out=K.ar(K.A_W1 + 7 * H2, H2),
                   in_=K.i_w1.ap()[l][7 * TILE:8 * TILE]))
        P.step("gpsimd", writes=[r_wo], dma=True,
               f=lambda e: e.dma_start(
                   out=K.ar(K.A_WO, 4 * 1024).rearrange("p (c n) -> p c n", c=4),
                   in_=K.i_wo.ap()[l].rearrange("(c p) n -> p c n", p=TILE)))
        P.step("sync", writes=[K.r_all(K.scl)], dma=True,
               f=lambda e: e.dma_start(out=K.scl[:, :], in_=K.i_scl.ap()[l]))
        r_lng = ("arena", K.A_LNG * 2, K.A_LNG * 2 + 1024 * 4)
        r_lnb = ("arena", K.A_LNG * 2 + 1024 * 4, K.A_LNG * 2 + 2048 * 4)
        P.step("sync", writes=[r_lng, r_lnb], dma=True,
               f=lambda e: e.dma_start(out=K.ar_f32(K.A_LNG, 2048),
                                       in_=K.i_lnrep.ap()[l]))
        lng = K.ar_f32(K.A_LNG, 2048)[:, 0:1024]
        lnb = K.ar_f32(K.A_LNG, 2048)[:, 1024:2048]
        P.step("vector", writes=[K.r_ar(K.A_XFM, 2 * 8 * 129)],
               f=lambda e: e.memset(K.ar(K.A_XFM, 2 * 8 * 129), 0.0))

        first = True
        pend_heads = None
        for tt in tiles:
            P.begin_capture()
            full = tt in full_tiles
            buf = tt % 2
            xin = K.tin[buf]
            r_xin = K.r_all(xin)
            row0 = tt * TILE
            P.step("sync", writes=[r_xin], dma=True,
                   f=lambda e, row0=row0, xin=xin: e.dma_start(
                       out=xin[:, :], in_=src.ap()[row0:row0 + TILE, :]))
            rtb = K.rt[buf]
            r_rtb = K.r_all(rtb)
            P.step("sync", writes=[r_rtb], dma=True,
                   f=lambda e, row0=row0, rtb=rtb: e.dma_start(
                       out=rtb[:, :], in_=K.i_rot.ap()[row0:row0 + TILE, :]))
            if l == 0:
                rc = buf * 2
                self._ln_stats([r_xin], xin[:, :], D, rc, rc + 1, par=buf)
                h2 = K.taux[buf]
                r_h2 = K.r_all(h2)
                P.step("scalar", reads=[r_xin, K.r_cols(K.stt, rc, rc + 2, 4)],
                       writes=[r_h2],
                       f=lambda e, xin=xin, h2=h2, rc=rc: e.activation(
                           h2[:, :], xin[:, :], AF.Identity,
                           scale=K.stt[:, rc:rc + 1], bias=K.stt[:, rc + 1:rc + 2]))
            else:
                h2 = xin
                r_h2 = r_xin
            hbf = K.tbf[buf]
            r_hbf = K.r_all(hbf)
            P.step("vector", reads=[r_h2], writes=[r_hbf],
                   f=lambda e, h2=h2, hbf=hbf: e.tensor_copy(hbf[:, :], h2[:, :]))
            slot = tt % 2
            xo = K.A_XFM + slot * 8 * 129
            xo_prev = K.A_XFM + (1 - slot) * 8 * 129
            r_xfm = K.r_ar(xo, 8 * 129)
            xfm3 = K.ar(xo, 8 * 129).rearrange("p (c n) -> p c n", c=8)
            if not first:
                P.step("vector", reads=[K.r_ar(xo_prev, 8 * 129)], writes=[r_xfm],
                       f=lambda e, xo=xo, xo_prev=xo_prev: e.tensor_copy(
                           K.ar(xo, 8 * 129).rearrange("p (c n) -> p c n", c=8)[:, :, 0:1],
                           K.ar(xo_prev, 8 * 129).rearrange("p (c n) -> p c n", c=8)[:, :, 128:129]))
            else:
                P.step("vector", writes=[r_xfm],
                       f=lambda e, xo=xo: e.memset(
                           K.ar(xo, 8 * 129).rearrange("p (c n) -> p c n", c=8)[:, :, 0:1], 0.0))
            def tr(e, hbf=hbf, xfm3=xfm3):
                last = None
                for c in range(8):
                    last = e.transpose(K.pt0[:, c * TILE:(c + 1) * TILE],
                                       hbf[:, c * TILE:(c + 1) * TILE], K.ident[:, :])
                return last
            P.step("tensor", reads=[r_hbf, K.r_all(K.ident)], writes=[K.r_all(K.pt0)], f=tr)
            P.step("vector", reads=[K.r_all(K.pt0)], writes=[r_xfm],
                   f=lambda e, xfm3=xfm3: e.tensor_copy(
                       xfm3[:, :, 1:129],
                       K.pt0[:, :].rearrange("p (c n) -> p c n", c=8)))
            for g in range(3):
                pb = (K.pb0, K.pb1)[g % 2]
                r_pb = K.r_all(pb)
                def mm(e, g=g, xo=xo, pb=pb):
                    x3 = K.ar(xo, 8 * 129).rearrange("p (c n) -> p c n", c=8)
                    last = None
                    for c in range(8):
                        lhs = x3[:, c, 1:129] if c < 4 else x3[:, c, 0:128]
                        last = e.matmul(pb[:, :], lhs, wqkv(c, g * 512, 512),
                                        start=(c == 0), stop=(c == 7))
                    return last
                P.step("tensor", reads=[r_xfm, r_wqkv], writes=[r_pb], f=mm)
                if K.dbg and g == 0 and full:
                    P.step("vector", reads=[r_pb], writes=[K.r_all(K.gel)],
                           f=lambda e, pb=pb: e.tensor_copy(K.gel[:, :], pb[:, :]))
                    P.step("gpsimd", reads=[K.r_all(K.gel)], dma=True,
                           writes=[("dbgbuf", 1, 2)],
                           f=lambda e, tt=tt: e.dma_start(
                               out=K.d_dbg.ap()[tt * TILE:(tt + 1) * TILE, 3880:4904]
                               .bitcast(mybir.dt.float32),
                               in_=K.gel[:, :]))
                if g == 2:
                    vs = K.A_VSP + (tt % 6) * 520
                    r_vs = ("vsp_slot", tt % 6, tt % 6 + 1)
                    P.step("vector", reads=[r_pb], writes=[r_vs],
                           f=lambda e, vs=vs, pb=pb: e.tensor_copy(
                               K.ar(vs, 520).rearrange("p (h c) -> p h c", c=65)[:, :, 0:64],
                               pb[:, :].rearrange("p (h c) -> p h c", c=64)))
                    P.step("vector", reads=[], writes=[r_vs], tag="vone",
                           f=lambda e, vs=vs: e.memset(
                               K.ar(vs, 520).rearrange("p (h c) -> p h c", c=65)[:, :, 64:65], 1.0))
                else:
                    sc = (8 if g == 0 else 16) + 32 * buf
                    gel = (K.gel, K.gel2)[buf]
                    r_gel = K.r_all(gel)
                    r_ss = K.r_cols(K.stt, sc, sc + 8, 4)
                    P.step("scalar", reads=[r_pb], writes=[r_gel],
                           f=lambda e, pb=pb, gel=gel: e.activation(
                               gel[:, :], pb[:, :], AF.Square))
                    P.step("vector", reads=[r_gel], writes=[r_ss],
                           f=lambda e, sc=sc, gel=gel: e.reduce_sum(
                               K.stt[:, sc:sc + 8],
                               gel[:, :].rearrange("p (h c) -> p h c", c=64),
                               axis=AX.X))
                    if K.dbg and g == 0 and full:
                        P.step("gpsimd", reads=[r_ss], dma=True,
                               writes=[("dbgbuf", 1, 2)],
                               f=lambda e, tt=tt, sc=sc: e.dma_start(
                                   out=K.d_dbg.ap()[tt * TILE:(tt + 1) * TILE,
                                                    4904:4920]
                                   .bitcast(mybir.dt.float32),
                                   in_=K.stt[:, sc:sc + 8]))
                    P.step("vector", reads=[r_ss], writes=[r_ss],
                           f=lambda e, sc=sc: e.tensor_scalar(
                               K.stt[:, sc:sc + 8], K.stt[:, sc:sc + 8],
                               1e-24, None, OP.max))
                    P.step("scalar", reads=[r_ss], writes=[r_ss],
                           f=lambda e, sc=sc: e.activation(
                               K.stt[:, sc:sc + 8], K.stt[:, sc:sc + 8], AF.Ln))
                    P.step("scalar", reads=[r_ss], writes=[r_ss],
                           f=lambda e, sc=sc: e.activation(
                               K.stt[:, sc:sc + 8], K.stt[:, sc:sc + 8],
                               AF.Exp, scale=-0.5))
                    so = 0 if g == 0 else 128
                    ro = 0 if g == 0 else 128
                    po = 0 if g == 0 else 128
                    qfac = QK_SCALE if g == 0 else 1.0
                    rpr = (K.rprep, K.rprep2)[buf]
                    r_rp = K.r_cols(rpr, po, po + 128, 4)
                    P.step("vector", reads=[r_rtb, K.r_all(K.scl)], writes=[r_rp],
                           f=lambda e, so=so, ro=ro, po=po, qfac=qfac, rtb=rtb, rpr=rpr:
                           e.scalar_tensor_tensor(
                               out=rpr[:, po:po + 64], in0=rtb[:, ro:ro + 64],
                               scalar=qfac, in1=K.scl[:, so:so + 64],
                               op0=OP.mult, op1=OP.mult))
                    P.step("vector", reads=[r_rtb, K.r_all(K.scl)], writes=[r_rp],
                           f=lambda e, so=so, ro=ro, po=po, qfac=qfac, rtb=rtb, rpr=rpr:
                           e.scalar_tensor_tensor(
                               out=rpr[:, po + 64:po + 128], in0=rtb[:, ro + 64:ro + 128],
                               scalar=qfac, in1=K.scl[:, so + 64:so + 128],
                               op0=OP.mult, op1=OP.mult))
                    qa, qb, qcv = (K.qc[0], K.qc[1], K.qc[2]) if buf == 0 else \
                        (K.qc[3], K.qc[4], K.qc[5])
                    r_qa, r_qb, r_qc2 = K.r_all(qa), K.r_all(qb), K.r_all(qcv)
                    P.step("scalar", reads=[r_pb], writes=[r_qa],
                           f=lambda e, pb=pb, qa=qa: e.copy(qa[:, :], pb[:, :]))
                    P.step("vector", reads=[r_qa, r_rp], writes=[r_qb],
                           f=lambda e, po=po, qa=qa, qb=qb, rpr=rpr: e.tensor_tensor(
                               qb[:, :].rearrange("p (h c) -> p h c", c=64),
                               qa[:, :].rearrange("p (h c) -> p h c", c=64),
                               rpr[:, po:po + 64].unsqueeze(1)
                               .broadcast_to([TILE, 8, 64]),
                               OP.mult))
                    def rot(e, qa=qa, qcv=qcv):
                        p3 = qa[:, :].rearrange("p (h c) -> p h c", c=64)
                        h3 = qcv[:, :].rearrange("p (h c) -> p h c", c=64)
                        e.tensor_scalar(h3[:, :, 0:32], p3[:, :, 32:64], -1.0, None, OP.mult)
                        return e.tensor_copy(h3[:, :, 32:64], p3[:, :, 0:32])
                    P.step("vector", reads=[r_qa], writes=[r_qc2], f=rot)
                    P.step("vector", reads=[r_qc2, r_rp], writes=[r_qa],
                           f=lambda e, po=po, qcv=qcv, qa=qa, rpr=rpr: e.tensor_tensor(
                               qa[:, :].rearrange("p (h c) -> p h c", c=64),
                               qcv[:, :].rearrange("p (h c) -> p h c", c=64),
                               rpr[:, po + 64:po + 128].unsqueeze(1)
                               .broadcast_to([TILE, 8, 64]),
                               OP.mult))
                    P.step("vector", reads=[r_qa, r_qb], writes=[r_qb],
                           f=lambda e, qa=qa, qb=qb: e.tensor_tensor(
                               qb[:, :], qb[:, :], qa[:, :], OP.add))
                    dst_rot = K.qrotB_[buf] if g == 0 else K.krotB_[buf]
                    r_drot = K.r_all(dst_rot)
                    P.step("vector", reads=[r_qb, r_ss], writes=[r_drot],
                           f=lambda e, sc=sc, qb=qb, dst_rot=dst_rot: e.tensor_tensor(
                               dst_rot[:, :].rearrange("p (h c) -> p h c", c=64),
                               qb[:, :].rearrange("p (h c) -> p h c", c=64),
                               K.stt[:, sc:sc + 8].unsqueeze(2)
                               .broadcast_to([TILE, 8, 64]),
                               OP.mult))
            slot6 = tt % 6
            def trk(e, krotB=K.krotB_[buf]):
                last = None
                for h in range(NH):
                    last = e.transpose(K.pt1[0:64, h * TILE:(h + 1) * TILE],
                                       krotB[:, h * 64:(h + 1) * 64], K.ident[:, :])
                return last
            P.step("tensor", reads=[K.r_all(K.krotB_[buf]), K.r_all(K.ident)],
                   writes=[K.r_all(K.pt1)], f=trk)
            r_kslot = ("kfm_slot", slot6, slot6 + 1)
            P.step("vector", reads=[K.r_all(K.pt1)], writes=[r_kslot],
                   f=lambda e, slot6=slot6: e.tensor_copy(
                       K.arena[0:64, K.A_KFM:K.A_KFM + 8 * 768]
                       .rearrange("p (g n) -> p g n", g=8)
                       [:, :, slot6 * TILE:(slot6 + 1) * TILE],
                       K.pt1[0:64, 0:1024].rearrange("p (c n) -> p c n", c=8)))
            if full:
                def trq(e, qrotB=K.qrotB_[buf]):
                    last = None
                    for h in range(NH):
                        last = e.transpose(K.pt0[0:64, h * TILE:(h + 1) * TILE],
                                           qrotB[:, h * 64:(h + 1) * 64],
                                           K.ident[:, :])
                    return last
                P.step("tensor", reads=[K.r_all(K.qrotB_[buf]), K.r_all(K.ident)],
                       writes=[K.r_all(K.pt0)], f=trq)
                P.step("vector", reads=[K.r_all(K.pt0)],
                       writes=[K.r_all(K.qfmB_[buf])],
                       f=lambda e, qfmB=K.qfmB_[buf]: e.tensor_copy(
                           qfmB[0:64, :], K.pt0[0:64, 0:1024]))
                m01 = K.m01[buf]
                r_m01 = K.r_all(m01)
                P.step("sync", writes=[r_m01], dma=True,
                       f=lambda e, tt=tt, m01=m01: e.dma_start(
                           out=m01[:, :], in_=K.i_m01.ap()[tt]))
                P.mark_split()
                qfmB = K.qfmB_[buf]
                def emit_sim(h, tt=tt, qfmB=qfmB):
                    psim = (K.psA, K.psB)[h % 2]
                    r_ps0, r_ps1 = K.r_bank(psim, 0), K.r_bank(psim, 1)
                    def sim(e, h=h, psim=psim, tt=tt, qfmB=qfmB):
                        segs = []
                        i = 0
                        while i < 5:
                            s0 = (tt - 4 + i) % 6
                            n = 1
                            while i + n < 5 and s0 + n <= 5:
                                n += 1
                            segs.append((i, s0, n))
                            i += n
                        final = []
                        for (i0, s0, n) in segs:
                            if i0 * TILE < 512 < (i0 + n) * TILE:
                                k = (512 - i0 * TILE) // TILE
                                final.append((i0, s0, k))
                                final.append((i0 + k, s0 + k, n - k))
                            else:
                                final.append((i0, s0, n))
                        last = None
                        for (i0, s0, n) in final:
                            rhs = K.arena[0:64,
                                          K.A_KFM + h * 768 + s0 * TILE:
                                          K.A_KFM + h * 768 + (s0 + n) * TILE]
                            last = e.matmul(psim[:, i0 * TILE:(i0 + n) * TILE],
                                            qfmB[0:64, h * TILE:(h + 1) * TILE],
                                            rhs, start=True, stop=True)
                        return last
                    kslots = [("kfm_slot", (tt - 4 + i) % 6, (tt - 4 + i) % 6 + 1)
                              for i in range(5)]
                    P.step("tensor", reads=[K.r_all(qfmB)] + kslots,
                           writes=[r_ps0, r_ps1], f=sim)

                def emit_front(h, m01=m01, r_m01=r_m01, buf=buf):
                    psim = (K.psA, K.psB)[h % 2]
                    r_ps0, r_ps1 = K.r_bank(psim, 0), K.r_bank(psim, 1)
                    hp = 24 + 4 * (h % 2)
                    P.step("vector", reads=[r_ps0, r_ps1],
                           writes=[K.r_cols(K.stt, hp, hp + 1, 4)],
                           f=lambda e, psim=psim, hp=hp: e.reduce_max(
                               K.stt[:, hp:hp + 1], psim[:, 0:640],
                               axis=AX.X, negate=True))
                    pb_ = K.pbf[h % 2]
                    r_pb_ = K.r_all(pb_)
                    P.step("scalar", reads=[r_ps0, r_ps1, K.r_cols(K.stt, hp, hp + 1, 4)],
                           writes=[r_pb_],
                           f=lambda e, psim=psim, pb_=pb_, hp=hp: e.activation(
                               pb_[:, :], psim[:, 0:640], AF.Exp,
                               bias=K.stt[:, hp:hp + 1]))
                    p2 = K.ptsb[h % 2]
                    r_p2 = K.r_all(p2)
                    P.step("vector", reads=[r_pb_, r_m01], writes=[r_p2],
                           f=lambda e, pb_=pb_, p2=p2, m01=m01: e.tensor_tensor(
                               p2[:, :], pb_[:, :], m01[:, :], OP.mult))

                def emit_back(h, tt=tt, buf=buf):
                    hp = 24 + 4 * (h % 2)
                    pb_ = K.pbf[h % 2]
                    r_pb_ = K.r_all(pb_)
                    p2 = K.ptsb[h % 2]
                    r_p2 = K.r_all(p2)
                    pt = (K.pt0, K.pt1)[h % 2]
                    def trp(e, p2=p2, pt=pt):
                        last = None
                        for c in range(5):
                            last = e.transpose(pt[:, c * TILE:(c + 1) * TILE],
                                               p2[:, c * TILE:(c + 1) * TILE],
                                               K.ident[:, :])
                        return last
                    P.step("tensor", reads=[r_p2, K.r_all(K.ident)],
                           writes=[K.r_all(pt)], f=trp)
                    P.step("scalar", reads=[K.r_all(pt)], writes=[r_pb_],
                           f=lambda e, pb_=pb_, pt=pt: e.copy(pb_[:, 0:640],
                                                              pt[:, 0:640]))
                    psim = (K.psA, K.psB)[h % 2]
                    nd = psim[:, 768:833]
                    r_nd = K.r_bank(psim, 1)
                    def av(e, h=h, pb_=pb_, nd=nd, tt=tt):
                        last = None
                        for i in range(5):
                            vs = K.A_VSP + ((tt - 4 + i) % 6) * 520
                            last = e.matmul(nd[:, 0:65],
                                            pb_[:, i * TILE:(i + 1) * TILE],
                                            K.ar(vs, 520)[:, h * 65:(h + 1) * 65],
                                            start=(i == 0), stop=(i == 4))
                        return last
                    vslots = [("vsp_slot", (tt - 4 + i) % 6, (tt - 4 + i) % 6 + 1)
                              for i in range(5)]
                    P.step("tensor", reads=[r_pb_] + vslots,
                           writes=[r_nd], f=av)
                    P.step("vector", reads=[r_nd],
                           writes=[K.r_cols(K.stt, hp + 1, hp + 2, 4)],
                           f=lambda e, nd=nd, hp=hp: e.tensor_scalar(
                               K.stt[:, hp + 1:hp + 2], nd[:, 64:65],
                               1e-30, None, OP.max))
                    P.step("vector", reads=[K.r_cols(K.stt, hp + 1, hp + 2, 4)],
                           writes=[K.r_cols(K.stt, hp + 2, hp + 3, 4)],
                           f=lambda e, hp=hp: e.reciprocal(
                               K.stt[:, hp + 2:hp + 3], K.stt[:, hp + 1:hp + 2]))
                    P.step("vector", reads=[r_nd, K.r_cols(K.stt, hp + 2, hp + 3, 4)],
                           writes=[K.r_cols(K.otmB_[buf], h * 64, (h + 1) * 64, 2)],
                           f=lambda e, h=h, nd=nd, hp=hp, otmB=K.otmB_[buf]:
                           e.tensor_tensor(
                               otmB[:, h * 64:(h + 1) * 64], nd[:, 0:64],
                               K.stt[:, hp + 2:hp + 3].broadcast_to([TILE, 64]),
                               OP.mult))

                emit_sim(0)
                emit_sim(1)
                for h in range(NH):
                    emit_front(h)
                    if h + 2 < NH:
                        emit_sim(h + 2)
                    emit_back(h)
                def tro(e, otmB=K.otmB_[buf]):
                    last = None
                    for c in range(4):
                        last = e.transpose(K.pt1[:, c * TILE:(c + 1) * TILE],
                                           otmB[:, c * TILE:(c + 1) * TILE],
                                           K.ident[:, :])
                    return last
                P.step("tensor", reads=[K.r_all(K.otmB_[buf]), K.r_all(K.ident)],
                       writes=[K.r_all(K.pt1)], f=tro)
                P.step("vector", reads=[K.r_all(K.pt1)],
                       writes=[K.r_all(K.ofmB_[buf])],
                       f=lambda e, ofmB=K.ofmB_[buf]: e.tensor_copy(
                           ofmB[:, :], K.pt1[:, 0:512]))
                for g in range(2):
                    pb = (K.pb0, K.pb1)[g]
                    r_pb = K.r_all(pb)
                    def womm(e, g=g, pb=pb, ofmB=K.ofmB_[buf]):
                        last = None
                        for c in range(4):
                            last = e.matmul(pb[:, :], ofmB[:, c * TILE:(c + 1) * TILE],
                                            wo_w(c, g * 512, 512),
                                            start=(c == 0), stop=(c == 3))
                        return last
                    P.step("tensor", reads=[K.r_all(K.ofmB_[buf]), r_wo],
                           writes=[r_pb], f=womm)
                    P.step("vector", reads=[r_pb, r_h2], writes=[r_h2],
                           f=lambda e, g=g, pb=pb, h2=h2: e.tensor_tensor(
                               h2[:, g * 512:(g + 1) * 512], h2[:, g * 512:(g + 1) * 512],
                               pb[:, :], OP.add))
                P.step("sync", reads=[r_h2],
                       writes=[("hbrow", buf * 2048, buf * 2048 + 2048)], dma=True,
                       f=lambda e, h2=h2, buf=buf: e.dma_start(
                           out=K.hbrow[0:1, buf * 512:buf * 512 + 512],
                           in_=h2[127:128, 512:1024]))
                P.step("gpsimd", reads=[r_h2], dma=True,
                       writes=[K.r_rows(dst, tt * TILE, (tt + 1) * TILE, rowb)],
                       f=lambda e, tt=tt, h2=h2: e.dma_start(
                           out=dst.ap()[tt * TILE:(tt + 1) * TILE, :], in_=h2[:, :]))
                hs = K.hsx[buf]
                r_hs = ("arena", (K.A_HSX + buf * 2048) * 2,
                        (K.A_HSX + buf * 2048 + 2048) * 2)
                r_hs_a = ("arena", (K.A_HSX + buf * 2048) * 2,
                          (K.A_HSX + buf * 2048 + 1024) * 2)
                r_hs_b = ("arena", (K.A_HSX + buf * 2048 + 1024) * 2,
                          (K.A_HSX + buf * 2048 + 2048) * 2)
                P.step("vector", reads=[r_h2], writes=[r_hs_a],
                       f=lambda e, h2=h2, hs=hs: e.tensor_copy(hs[:, 0:512],
                                                               h2[:, 0:512]))
                P.step("sync", reads=[r_h2], writes=[r_hs_b],
                       dma=True,
                       f=lambda e, h2=h2, hs=hs: e.dma_start(
                           out=hs[1:128, 512:1024], in_=h2[0:127, 512:1024]))
                if tt == tiles[len(kv_tiles)]:
                    P.step("vector", writes=[r_hs_b],
                           f=lambda e, hs=hs: e.memset(hs[0:1, 512:1024], 0.0))
                else:
                    pv = (1 - buf) * 2048
                    P.step("sync",
                           reads=[("hbrow", pv, pv + 2048)],
                           writes=[r_hs_b], dma=True,
                           f=lambda e, hs=hs, buf=buf: e.dma_start(
                               out=hs[0:1, 512:1024],
                               in_=K.hbrow[0:1, (1 - buf) * 512:(1 - buf) * 512 + 512]))
                rcf = 56 + 2 * buf
                self._ln_stats([r_hs], hs[:, :], D, rcf, rcf + 1, par=8 + buf)
                P.step("scalar", reads=[r_hs, K.r_cols(K.stt, rcf, rcf + 2, 4)],
                       writes=[r_hs],
                       f=lambda e, hs=hs, rcf=rcf: e.activation(
                           hs[:, :], hs[:, :], AF.Identity,
                           scale=K.stt[:, rcf:rcf + 1], bias=K.stt[:, rcf + 1:rcf + 2]))
                P.step("vector", reads=[r_hs, r_lng], writes=[r_hs],
                       f=lambda e, hs=hs: e.tensor_tensor(hs[:, :], hs[:, :], lng,
                                                          OP.mult))
                hl = K.tbf[buf]
                r_hl = K.r_all(hl)
                P.step("vector", reads=[r_hs, r_lnb], writes=[r_hl],
                       f=lambda e, hs=hs, hl=hl: e.tensor_tensor(hl[:, :], hs[:, :],
                                                                 lnb, OP.add))
                P.step("gpsimd", reads=[r_hl], dma=True,
                       writes=[K.r_rows(K.d_hln, tt * TILE, (tt + 1) * TILE, D * 2)],
                       f=lambda e, tt=tt, hl=hl: e.dma_start(
                           out=K.d_hln.ap()[tt * TILE:(tt + 1) * TILE, :],
                           in_=hl[:, :]))
            first = False
            front, heads = P.capture()
            if pend_heads and K.il_lag is not None:
                P.emit_interleaved(pend_heads, front, lag=K.il_lag)
            else:
                if pend_heads:
                    P.steps.extend(pend_heads)
                P.steps.extend(front)
            pend_heads = heads
        if pend_heads:
            P.steps.extend(pend_heads)
        P.step("gpsimd", writes=[K.r_ar(K.A_W1, 7 * H2),
                                 ("kfm_slot", 0, 6), ("vsp_slot", 0, 6)], dma=True,
               f=lambda e: e.dma_start(
                   out=K.ar(K.A_W1, 7 * H2).rearrange("p (c n) -> p c n", c=7),
                   in_=K.i_w1.ap()[l][0:7 * TILE].rearrange("(c p) n -> p c n",
                                                            p=TILE)))

    def ph_Bf(self, l):
        P, K = self.P, self
        rowb = D * 4
        rowb2 = D * 2
        src = K.d_sa
        dst = K.d_sb
        tiles = list(B0F if l == 0 else B1F)
        t_lo = tiles[0]
        P.phase(f"B{l}f_g")
        P.step("gpsimd", writes=[K.r_all(K.b1_bf)], dma=True,
               f=lambda e: e.dma_start(out=K.b1_bf[:, :], in_=K.i_b1.ap()[l]))
        P.step("gpsimd", writes=[K.r_all(K.b2_bf)], dma=True,
               f=lambda e: e.dma_start(out=K.b2_bf[:, :], in_=K.i_b2.ap()[l]))
        r_w1 = K.r_ar(K.A_W1, 8 * H2)
        r_w2 = K.r_ar(K.A_W2, 22 * 1024)
        w1w = lambda c, lo, n: K.ar(K.A_W1 + c * H2 + lo, n)
        w2w = lambda c, lo, n: K.ar(K.A_W2 + c * 1024 + lo, n)
        chunks = [(i * 512, min(512, HID - i * 512)) for i in range(cdiv(HID, 512))]
        for tt in tiles:
            buf = tt % 2
            row0 = tt * TILE
            hl = K.tbf[buf]
            r_hl = K.r_all(hl)
            P.step("sync", writes=[r_hl], dma=True,
                   f=lambda e, row0=row0, hl=hl: e.dma_start(
                       out=hl[:, :], in_=K.d_hln.ap()[row0:row0 + TILE, :]))
            xm = K.tin[buf]
            r_xm = K.r_all(xm)
            P.step("sync", writes=[r_xm], dma=True,
                   f=lambda e, row0=row0, xm=xm: e.dma_start(
                       out=xm[:, :], in_=src.ap()[row0:row0 + TILE, :]))
            mr = K.mrow[buf]
            r_mr = K.r_all(mr)
            P.step("sync", writes=[r_mr], dma=True,
                   f=lambda e, row0=row0, mr=mr: e.dma_start(
                       out=mr[:, :], in_=K.i_rmask.ap()[PRE + row0:PRE + row0 + TILE, :]))
            hfm = K.h1fm[buf]
            r_hfm = K.r_all(hfm)
            self._transpose8(hl, [r_hl], K.pt0, hfm[:, :], [r_hfm])
            r_pA0, r_pA1 = K.r_bank(K.psA, 0), K.r_bank(K.psA, 1)
            for ci, (c0, cn) in enumerate(chunks):
                if ci % 2 == 0:
                    pa, pg = K.pb0[:, :], K.pb1[:, :]
                    r_pa, r_pg = K.r_all(K.pb0), K.r_all(K.pb1)
                else:
                    pa, pg = K.psB[:, 0:512], K.psB[:, 512:1024]
                    r_pa, r_pg = K.r_bank(K.psB, 0), K.r_bank(K.psB, 1)
                gfmB = K.gfmB_[ci % 2]
                r_gfm = K.r_all(gfmB)
                def mma(e, c0=c0, cn=cn, hfm=hfm, pa=pa):
                    for c in range(8):
                        e.matmul(pa[:, 0:cn], hfm[:, c * TILE:(c + 1) * TILE],
                                 w1w(c, c0, cn), start=(c == 0), stop=False)
                    return e.matmul(pa[:, 0:cn], K.ones_row[:, :],
                                    K.b1_bf[:, c0:c0 + cn], start=False, stop=True)
                P.step("tensor", reads=[r_hfm, r_w1, K.r_all(K.b1_bf),
                                        K.r_all(K.ones_row)], writes=[r_pa], f=mma)
                def mmg(e, c0=c0, cn=cn, hfm=hfm, pg=pg):
                    for c in range(8):
                        e.matmul(pg[:, 0:cn], hfm[:, c * TILE:(c + 1) * TILE],
                                 w1w(c, HID + c0, cn), start=(c == 0), stop=False)
                    return e.matmul(pg[:, 0:cn], K.ones_row[:, :],
                                    K.b1_bf[:, HID + c0:HID + c0 + cn],
                                    start=False, stop=True)
                P.step("tensor", reads=[r_hfm, r_w1, K.r_all(K.b1_bf),
                                        K.r_all(K.ones_row)], writes=[r_pg], f=mmg)
                gelx = (K.gel, K.gel2)[ci % 2]
                glbf = (K.glbf, K.glbf2)[ci % 2]
                P.step("scalar", reads=[r_pg], writes=[K.r_all(gelx)],
                       f=lambda e, cn=cn, gelx=gelx, pg=pg: e.activation(
                           gelx[:, 0:cn], pg[:, 0:cn], AF.Gelu))
                P.step("vector", reads=[r_pa, K.r_all(gelx)], writes=[K.r_all(glbf)],
                       f=lambda e, cn=cn, gelx=gelx, pa=pa, glbf=glbf: e.tensor_tensor(
                           glbf[:, 0:cn], pa[:, 0:cn], gelx[:, 0:cn], OP.mult))
                if K.dbg and ci == 0:
                    P.step("gpsimd", reads=[K.r_all(K.glbf)], dma=True,
                           writes=[("dbgbuf", 1, 2)],
                           f=lambda e, tt=tt: e.dma_start(
                               out=K.d_dbg.ap()[tt * TILE:(tt + 1) * TILE, 0:512],
                               in_=K.glbf[:, :]))
                    P.step("gpsimd", reads=[K.r_all(K.gel)], dma=True,
                           writes=[("dbgbuf", 1, 2)],
                           f=lambda e, tt=tt: e.dma_start(
                               out=K.d_dbg.ap()[tt * TILE:(tt + 1) * TILE, 512:1536]
                               .bitcast(mybir.dt.float32),
                               in_=K.gel[:, :]))
                nch = cdiv(cn, TILE)
                ptx = (K.pt1, K.pt0)[ci % 2]
                def trg(e, cn=cn, nch=nch, glbf=glbf, ptx=ptx):
                    last = None
                    for c in range(nch):
                        w = min(TILE, cn - c * TILE)
                        last = e.transpose(ptx[0:w, c * TILE:(c + 1) * TILE],
                                           glbf[:, c * TILE:c * TILE + w],
                                           K.ident[:, :])
                    return last
                P.step("tensor", reads=[K.r_all(glbf), K.r_all(K.ident)],
                       writes=[K.r_all(ptx)], f=trg)
                P.step("vector", reads=[K.r_all(ptx)], writes=[r_gfm],
                       f=lambda e, nch=nch, gfmB=gfmB, ptx=ptx: e.tensor_copy(
                           gfmB[:, 0:nch * TILE], ptx[:, 0:nch * TILE]))
                def mw2b(e, ci=ci, c0=c0, cn=cn, nch=nch, gfmB=gfmB):
                    last = None
                    for c in range(nch):
                        w = min(TILE, cn - c * TILE)
                        cc = (c0 + c * TILE) // TILE
                        for g in range(2):
                            last = e.matmul(
                                K.psA[:, g * 512:(g + 1) * 512],
                                gfmB[0:w, c * TILE:(c + 1) * TILE],
                                K.ar(K.A_W2 + cc * 1024 + g * 512, 512)[0:w, :],
                                start=(ci == 0 and c == 0), stop=False)
                    return last
                P.step("tensor", reads=[r_gfm, r_w2],
                       writes=[r_pA0, r_pA1], f=mw2b)
            P.step("tensor", reads=[K.r_all(K.ones_row), K.r_all(K.b2_bf)],
                   writes=[r_pA0, r_pA1],
                   f=lambda e: (e.matmul(K.psA[:, 0:512], K.ones_row[:, :],
                                         K.b2_bf[:, 0:512], start=False, stop=True),
                                e.matmul(K.psA[:, 512:1024], K.ones_row[:, :],
                                         K.b2_bf[:, 512:1024], start=False, stop=True))[1])
            if K.dbg:
                P.step("vector", reads=[r_pA0], writes=[K.r_all(K.gel)],
                       f=lambda e: e.tensor_copy(K.gel[:, :], K.psA[:, 0:512]))
                P.step("gpsimd", reads=[K.r_all(K.gel)], dma=True,
                       writes=[("dbgbuf", 1, 2)],
                       f=lambda e, tt=tt: e.dma_start(
                           out=K.d_dbg.ap()[tt * TILE:(tt + 1) * TILE, 1536:2560]
                           .bitcast(mybir.dt.float32),
                           in_=K.gel[:, :]))
                P.step("gpsimd", reads=[K.r_all(K.gfmB_[1])], dma=True,
                       writes=[("dbgbuf", 1, 2)],
                       f=lambda e, tt=tt: e.dma_start(
                           out=K.d_dbg.ap()[tt * TILE:(tt + 1) * TILE, 2560:3072],
                           in_=K.gfmB_[1][:, :]))
            P.step("vector", reads=[r_pA0, r_pA1, r_xm, r_mr], writes=[r_xm],
                   f=lambda e, xm=xm, mr=mr: e.scalar_tensor_tensor(
                       out=xm[:, :], in0=K.psA[:, :], scalar=mr[:, 0:1],
                       in1=xm[:, :], op0=OP.mult, op1=OP.add))
            P.step("gpsimd", reads=[r_xm], dma=True,
                   writes=[K.r_rows(dst, tt * TILE, (tt + 1) * TILE, rowb)],
                   f=lambda e, tt=tt, xm=xm: e.dma_start(
                       out=dst.ap()[tt * TILE:(tt + 1) * TILE, :], in_=xm[:, :]))

    def ph_C(self):
        P, K = self.P, self
        P.phase("C")
        rowb = D * 4
        for i, tt in enumerate(CT):
            buf = tt % 2
            row0 = tt * TILE
            ft = K.tin[buf]
            r_ft = K.r_all(ft)
            P.step("sync", writes=[r_ft], dma=True,
                   f=lambda e, row0=row0, ft=ft: e.dma_start(
                       out=ft[:, :], in_=K.d_sb.ap()[row0:row0 + TILE, :]))
            x1 = K.taux[buf]
            r_x1 = K.r_all(x1)
            P.step("sync", writes=[r_x1], dma=True,
                   f=lambda e, row0=row0, x1=x1: e.dma_start(
                       out=x1[:, :], in_=K.d_x1.ap()[row0:row0 + TILE, :]))
            rc = buf * 2
            self._ln_stats([r_ft], ft[:, :], D, rc, rc + 1, par=buf)
            lnf = K.tbf[buf]
            lo = K.qc[0] if buf == 0 else K.qc[1]
            ob = K.taux[1 - buf]
            r_ob = K.r_all(ob)
            P.step("scalar", reads=[r_ft, K.r_cols(K.stt, rc, rc + 2, 4)],
                   writes=[r_ob],
                   f=lambda e, ft=ft, ob=ob, rc=rc: e.activation(
                       ob[:, :], ft[:, :], AF.Identity,
                       scale=K.stt[:, rc:rc + 1], bias=K.stt[:, rc + 1:rc + 2]))
            P.step("vector", reads=[r_ob, r_x1], writes=[r_ob],
                   f=lambda e, ob=ob, x1=x1: e.tensor_tensor(
                       ob[:, :], ob[:, :], x1[:, :], OP.add))
            P.step("gpsimd", reads=[r_ob], dma=True,
                   writes=[K.r_rows(K.o_y, i * TILE, (i + 1) * TILE, rowb)],
                   f=lambda e, i=i, ob=ob: e.dma_start(
                       out=K.o_y.ap()[i * TILE:(i + 1) * TILE, :], in_=ob[:, :]))
        P.step("gpsimd", reads=[K.r_rows(K.o_y, 0, OWN_T * TILE, rowb)],
               f=lambda e: e.nop())


def _rotary_tables():
    t = np.arange(8192, dtype=np.float64)
    half = np.arange(0, DH, 2, dtype=np.float64)
    inv_freq = 1.0 / 10000.0 ** (half / DH)
    freqs = t[:, None] * inv_freq[None, :]
    freqs = np.concatenate([freqs, freqs], -1)
    base = (half + 0.4 * DH) / (1.4 * DH)
    power = (t - 8192 // 2) / float(WIN // 2)
    scale = base[None, :] ** power[:, None]
    scale = np.concatenate([scale, scale], -1)
    cos, sin = np.cos(freqs), np.sin(freqs)
    return (np.float32(cos * scale), np.float32(sin * scale),
            np.float32(cos / scale), np.float32(sin / scale))


def _m01_masks(pad_rows):
    ii = np.arange(TILE)[:, None]
    jj = np.arange(640)[None, :]
    band = (jj >= ii) & (jj <= ii + WIN)
    m = np.zeros((SPAN_T, TILE, 640), dtype=np.float32)
    for tt in range(SPAN_T):
        kpos = tt * TILE - WIN + jj
        valid = kpos >= (pad_rows - PRE)
        m[tt] = (band & valid).astype(np.float32)
    return m.astype(BF)


def stage_inputs(L0_attn_x, L0_attn_qkv_w, L0_attn_qkv_b, L0_attn_out_w,
                 L0_attn_out_b, L0_ffn_wqkv, L0_ffn_q_scale, L0_ffn_k_scale,
                 L0_ffn_wo, L0_ffn_ln_g, L0_ffn_ln_b, L0_ffn_w1, L0_ffn_b1,
                 L0_ffn_w2, L0_ffn_b2):
    x0 = np.ascontiguousarray(np.asarray(L0_attn_x, dtype=np.float32))
    csq, ssq, csk, ssk = _rotary_tables()
    rot_full = np.concatenate([csq, ssq, csk, ssk], axis=1)

    rep = lambda v: np.ascontiguousarray(np.broadcast_to(
        np.asarray(v, np.float32)[None, :], (TILE, len(np.asarray(v)))))

    shared = {
        "qkvwT": np.ascontiguousarray(np.asarray(L0_attn_qkv_w, np.float32).T),
        "qkvb": np.asarray(L0_attn_qkv_b, np.float32).reshape(1, -1),
        "outwT": np.ascontiguousarray(np.asarray(L0_attn_out_w, np.float32).T),
        "outb": np.asarray(L0_attn_out_b, np.float32).reshape(1, -1),
        "wqkv": np.ascontiguousarray(np.asarray(L0_ffn_wqkv, np.float32)),
        "wo": np.ascontiguousarray(np.asarray(L0_ffn_wo, np.float32)),
        "w1": np.ascontiguousarray(np.asarray(L0_ffn_w1, np.float32)),
        "b1": np.ascontiguousarray(np.asarray(L0_ffn_b1, np.float32).reshape(2, 1, -1)),
        "w2": np.ascontiguousarray(np.asarray(L0_ffn_w2, np.float32)),
        "b2": np.ascontiguousarray(np.asarray(L0_ffn_b2, np.float32).reshape(2, 1, -1)),
        "cmaskI": (np.arange(TILE)[:, None] <= np.arange(TILE)[None, :]).astype(BF),
        "identI": np.eye(TILE, dtype=BF),
        "epsc": np.broadcast_to(
            np.array([LNEPS, 0, 0, 0, 0, 0, 0, 0], np.float32)[None, :],
            (TILE, 8)).copy(),
    }
    scl = np.zeros((2, TILE, 256), np.float32)
    lnrep = np.zeros((2, TILE, 2048), np.float32)
    qs = np.asarray(L0_ffn_q_scale, np.float32)
    ks = np.asarray(L0_ffn_k_scale, np.float32)
    g_ = np.asarray(L0_ffn_ln_g, np.float32)
    b_ = np.asarray(L0_ffn_ln_b, np.float32)
    for l in (0, 1):
        scl[l, :, 0:64] = rep(qs[l])
        scl[l, :, 64:128] = rep(np.roll(qs[l], 32))
        scl[l, :, 128:192] = rep(ks[l])
        scl[l, :, 192:256] = rep(np.roll(ks[l], 32))
        lnrep[l, :, 0:1024] = rep(g_[l])
        lnrep[l, :, 1024:2048] = rep(b_[l])
    shared["scl"] = scl
    shared["lnrep"] = lnrep

    in_maps = []
    for c in range(8):
        s, q = c // 4, c % 4
        t0 = q * 2048
        pad = 8192 - (t0 + 2048)
        xs = np.zeros((8192, D), np.float32)
        if pad < 8192:
            xs[pad:] = x0[s, 0:t0 + 2048]
        rm = np.zeros((8192, 1), np.float32)
        rm[pad:] = 1.0
        absr = np.clip(np.arange(SPAN) + t0 - HALO_T * TILE, 0, 8191)
        m = {k: v for k, v in shared.items()}
        m["x0stage"] = xs
        m["rmask"] = rm
        m["rot"] = np.ascontiguousarray(rot_full[absr])
        m["m01"] = _m01_masks(pad)
        in_maps.append(m)
    return in_maps


def kernel(**inputs):
    in_maps = stage_inputs(**inputs)
    with ExitStack() as es:
        kb = KB(es)
        nc = kb.build()
        res = run_bass_kernel_spmd(nc, in_maps, core_ids=list(range(8)))
    out = np.empty((2, 8192, D), np.float32)
    for c in range(8):
        s, q = c // 4, c % 4
        out[s, q * 2048:(q + 1) * 2048] = res.results[c]["y"]
    return out


# version 13
# speedup vs baseline: 1.1230x; 1.0114x over previous
import numpy as np
import ml_dtypes
from contextlib import ExitStack

import concourse.bass as bass
import concourse.mybir as mybir
from concourse.bass_utils import run_bass_kernel_spmd
from concourse.alu_op_type import AluOpType as OP

F32 = mybir.dt.float32
BF16 = mybir.dt.bfloat16
AF = mybir.ActivationFunctionType
AX = mybir.AxisListType

BF = ml_dtypes.bfloat16

D = 1024
TILE = 128
HALO_T = 9
OWN_T = 16
SPAN_T = HALO_T + OWN_T
PRE_T = (8192 - SPAN_T * TILE) // TILE
SPAN = SPAN_T * TILE
PRE = PRE_T * TILE
LNEPS = 1e-5

LA_H = 16
LA_D = 64
NH = 8
DH = 64
INNER = 512
WIN = 512
HID = 2730
H2 = 2 * HID
QK_SCALE = 8.0

B0A_KV = range(0, 4)
B0A_FULL = range(4, SPAN_T)
B0F = range(4, SPAN_T)
B1A_KV = range(4, 8)
B1A_FULL = range(8, SPAN_T)
B1F = range(HALO_T, SPAN_T)
CT = range(HALO_T, SPAN_T)

ENGS = ("sync", "tensor", "vector", "scalar", "gpsimd")


_STEP_SEQ = [0]


class Step:
    __slots__ = ("eng", "reads", "writes", "emits", "dma", "tok", "waits",
                 "needs_tok", "clock", "tag", "sid", "depw")

    def __init__(self, eng, reads, writes, emits, dma, tag):
        _STEP_SEQ[0] += 1
        self.sid = _STEP_SEQ[0]
        self.depw = None
        self.eng = eng
        self.reads = reads
        self.writes = writes
        self.emits = emits
        self.dma = dma
        self.tok = None
        self.waits = []
        self.needs_tok = False
        self.clock = None
        self.tag = tag


def _ov(a, b):
    return a[0] == b[0] and a[1] < b[3] and b[1] < a[3] if False else None


class Prog:

    def __init__(self, nc):
        self.nc = nc
        self.steps = []
        self.phases = []
        self._cur_phase = None
        self._cur_lo = 0
        self._events = {}

    def phase(self, name):
        if self._cur_phase is not None:
            self.phases.append((self._cur_phase, self._cur_lo, len(self.steps)))
        self._cur_phase = name
        self._cur_lo = len(self.steps)

    def step(self, eng, reads=(), writes=(), f=None, dma=False, tag=""):
        st = Step(eng, list(reads), list(writes), f, dma, tag)
        self.steps.append(st)
        return st

    def begin_capture(self):
        self._cap = len(self.steps)
        self._split = None
        self._split2 = None

    def mark_split(self):
        self._split = len(self.steps)

    def mark_split2(self):
        self._split2 = len(self.steps)

    def capture(self):
        g = self.steps[self._cap:]
        sp = (self._split - self._cap) if self._split is not None else len(g)
        del self.steps[self._cap:]
        return g[:sp], g[sp:]

    def capture3(self):
        g = self.steps[self._cap:]
        sp = (self._split - self._cap) if self._split is not None else len(g)
        sp2 = (self._split2 - self._cap) if self._split2 is not None else len(g)
        del self.steps[self._cap:]
        return g[:sp], g[sp:sp2], g[sp2:]

    def emit_interleaved(self, a, b, lag):
        self.steps.extend(a[:lag])
        ai, bi = lag, 0
        while ai < len(a) or bi < len(b):
            if ai < len(a):
                self.steps.append(a[ai]); ai += 1
            if bi < len(b):
                self.steps.append(b[bi]); bi += 1

    def finalize(self):
        if self._cur_phase is not None:
            self.phases.append((self._cur_phase, self._cur_lo, len(self.steps)))
            self._cur_phase = None
        deps_of = [None] * len(self.steps)
        for i, st in enumerate(self.steps):
            deps = set()
            depw = set()
            for (name, lo, hi) in st.reads:
                for (elo, ehi, es, wr) in self._events.get(name, ()):
                    if wr and elo < hi and lo < ehi:
                        deps.add(es)
                        depw.add((name, lo, hi, self.steps[es].sid))
            for (name, lo, hi) in st.writes:
                for (elo, ehi, es, wr) in self._events.get(name, ()):
                    if elo < hi and lo < ehi:
                        deps.add(es)
            st.depw = depw
            deps_of[i] = deps
            for (name, lo, hi) in st.reads:
                self._events.setdefault(name, []).append((lo, hi, i, False))
            for (name, lo, hi) in st.writes:
                ev = self._events.setdefault(name, [])
                ev[:] = [e for e in ev if not (lo <= e[0] and e[1] <= hi)]
                ev.append((lo, hi, i, True))
        RROT = 8
        for i, st in enumerate(self.steps):
            for d in deps_of[i]:
                self.steps[d].needs_tok = True
        counters = {}
        dma_rr = {}
        for st in self.steps:
            if st.dma:
                rr = dma_rr.get(st.eng, 0)
                dma_rr[st.eng] = rr + 1
                key = f"{st.eng}_dma_{rr % RROT}"
                counters[key] = counters.get(key, 0) + 1
                st.tok = (key, counters[key])
            elif st.needs_tok:
                key = st.eng
                counters[key] = counters.get(key, 0) + 1
                st.tok = (key, counters[key])
        self.sem_keys = sorted(counters)
        watermark = {e: {} for e in ENGS}
        for i, st in enumerate(self.steps):
            wm = watermark[st.eng]
            if st.dma and st.tok[1] > 1:
                key, cnt = st.tok
                if wm.get(key, 0) < cnt - 1:
                    st.waits.append((key, cnt - 1))
                    wm[key] = cnt - 1
            dl = sorted(deps_of[i], key=lambda d: -(self.steps[d].tok[1]))
            for d in dl:
                dst = self.steps[d]
                key, cnt = dst.tok
                if wm.get(key, 0) >= cnt:
                    continue
                st.waits.append((key, cnt))
                wm[key] = cnt
                for k2, c2 in dst.clock.items():
                    if wm.get(k2, 0) < c2:
                        wm[k2] = c2
            cl = dict(wm)
            if st.tok is not None:
                cl[st.tok[0]] = st.tok[1]
            st.clock = cl
        return counters

    def emit(self, sem_handles):
        nc = self.nc
        for (pname, lo, hi) in self.phases:
            chunk = self.steps[lo:hi]
            per_eng = {e: [s for s in chunk if s.eng == e] for e in ENGS}
            with nc.Block(pname.replace("/", "_")) as blk:
                for ename in ENGS:
                    sts = per_eng[ename]
                    if not sts:
                        continue

                    def make(sts=sts):
                        def body(eng):
                            for st in sts:
                                for (key, cnt) in st.waits:
                                    v = cnt * 16 if "_dma_" in key else cnt
                                    eng.wait_ge(sem_handles[key], v)
                                inst = st.emits(eng)
                                if st.tok is not None:
                                    n = 16 if st.dma else 1
                                    if inst is None:
                                        inst = eng.nop()
                                    inst.then_inc(sem_handles[st.tok[0]], n)
                        return body

                    getattr(blk, ename)(make())


def cdiv(a, b):
    return (a + b - 1) // b


class KB:

    def __init__(self, es, dbg=False, a_limit=None, a_parts=63, il_lag=4):
        nc = bass.Bass("TRN2", target_bir_lowering=False)
        self.nc = nc
        self.es = es
        self.dbg = dbg
        self.a_limit = a_limit
        self.a_parts = a_parts
        self.il_lag = il_lag
        self.P = Prog(nc)
        self._mk_io()
        self._mk_sbuf()
        self._mk_psum()

    def r_cols(self, obj, lo, hi, itemsize):
        off = self._views.get(id(obj)) if hasattr(self, "_views") else None
        if off is not None:
            return ("arena", (off * 2) + lo * itemsize, (off * 2) + hi * itemsize)
        return (obj.name, lo * itemsize, hi * itemsize)

    def r_rows(self, handle, lo, hi, rowbytes):
        return (handle.name, lo * rowbytes, hi * rowbytes)

    def _mk_io(self):
        nc = self.nc
        di = lambda n, s, dt=F32: nc.dram_tensor(n, s, dt, kind="ExternalInput")
        self.i_x0 = di("x0stage", [8192, D])
        self.i_rmask = di("rmask", [8192, 1])
        self.i_m01 = di("m01", [SPAN_T, TILE, 640], BF16)
        self.i_rot = di("rot", [SPAN, 256])
        self.i_scl = di("scl", [2, TILE, 256])
        self.i_lnrep = di("lnrep", [2, TILE, 2048])
        self.i_cmask = di("cmaskI", [TILE, TILE], BF16)
        self.i_ident = di("identI", [TILE, TILE], BF16)
        self.i_epsc = di("epsc", [TILE, 8])
        self.i_qkvw = di("qkvwT", [D, 3 * D])
        self.i_qkvb = di("qkvb", [1, 3 * D])
        self.i_outw = di("outwT", [D, D])
        self.i_outb = di("outb", [1, D])
        self.i_wqkv = di("wqkv", [2, D, 3 * INNER])
        self.i_wo = di("wo", [2, INNER, D])
        self.i_w1 = di("w1", [2, D, H2])
        self.i_b1 = di("b1", [2, 1, H2])
        self.i_w2 = di("w2", [2, HID, D])
        self.i_b2 = di("b2", [2, 1, D])
        self.o_y = nc.dram_tensor("y", [OWN_T * TILE, D], F32, kind="ExternalOutput")
        dd = lambda n, s, dt=F32: nc.dram_tensor(n, s, dt, kind="Internal")
        self.d_x1 = dd("x1buf", [SPAN, D])
        self.d_sa = dd("sbufA", [SPAN, D])
        self.d_sb = dd("sbufB", [SPAN, D])
        self.d_hln = dd("hlnbuf", [SPAN, D], BF16)
        if self.dbg:
            self.d_dbg = dd("dbgbuf", [SPAN, 4 * 1024 + 1040], BF16)
            self.d_dbg2 = dd("dbgbuf2", [TILE, 8192], BF16)

    def _mk_sbuf(self):
        nc, es = self.nc, self.es
        sb = lambda n, s, dt: es.enter_context(nc.sbuf_tensor(n, s, dt))
        self.tin = [sb("tin0", [TILE, D], F32), sb("tin1", [TILE, D], F32)]
        self.taux = [sb("taux0", [TILE, D], F32), sb("taux1", [TILE, D], F32)]
        self.qc = [sb("qc0", [TILE, 512], F32), sb("qc1", [TILE, 512], F32),
                   sb("qc2", [TILE, 512], F32), sb("qc3", [TILE, 512], F32),
                   sb("qc4", [TILE, 512], F32), sb("qc5", [TILE, 512], F32)]
        self.gel = sb("gelf", [TILE, 512], F32)
        self.gel2 = sb("gelf2", [TILE, 512], F32)
        self.rt = [sb("rt0", [TILE, 256], F32), sb("rt1", [TILE, 256], F32)]
        self.rprep = sb("rprep", [TILE, 256], F32)
        self.rprep2 = sb("rprep2", [TILE, 256], F32)
        self.scl = sb("sclb", [TILE, 256], F32)
        self.stt = sb("stt", [TILE, 96], F32)
        self.strst = sb("strst", [TILE, SPAN_T], F32)
        self.stnmr = sb("stnmr", [TILE, SPAN_T], F32)
        self.epsc = sb("epscb", [TILE, 8], F32)
        self.mrow = [sb("mrow0", [TILE, 1], F32), sb("mrow1", [TILE, 1], F32)]
        self.tbf = [sb("tbf0", [TILE, D], BF16), sb("tbf1", [TILE, D], BF16)]
        self.cmask = sb("cmaskb", [TILE, TILE], BF16)
        self.ident = sb("identb", [TILE, TILE], BF16)
        self.ones_row = sb("ones_row", [1, TILE], BF16)
        self.zrow = sb("zrow", [1, 512], BF16)
        self.A_W1 = 0
        self.A_W2 = 43680
        self.A_B1R = 66208
        self.A_B2R = 71668
        self.A_WQKV = 0
        self.A_WO = 12288
        self.A_KFM = 16384
        self.A_VSP = 22528
        self.A_XFM = 25648
        self.A_LNG = 77300
        self.A_QFMB = 27712
        self.A_QROTB = 28736
        self.A_KROTB = 29248
        self.A_OTMB = 29760
        self.A_OFMB = 30272
        self.A_PBF = 30784
        self.A_PTSB = 32064
        self.A_M01 = 33344
        self.A_QFMB2 = 34624
        self.A_PBF2 = 35136
        self.A_PTSB2 = 36416
        self.A_QKVW = 0
        self.A_OUTW = 24576
        self.A_QKVBR = 32768
        self.A_OUTBR = 35840
        self.A_QTM = 36864
        self.A_KTM = 37888
        self.A_QFMA = 38912
        self.A_KFMA = 40960
        self.A_OTM = 43008
        self.A_OFM = 44032
        self.A_VTA = 45056
        self.A_AMSB = 46096
        self.A_SSB = 47120
        self.A_ZSB = 48144
        self.A_VTA2 = 48160
        self.A_QFMA2 = 49200
        self.A_KFMA2 = 51248
        self.A_OTM2 = 53296
        self.A_QROTB2 = 35136
        self.A_KROTB2 = 35648
        self.A_OTMB2 = 36160
        self.A_OFMB2 = 36672
        self.A_GLBF = 72692
        self.A_H1FM = 73204
        self.A_LNJ = 75252
        self.A_GFM = 81396
        self.A_HSX = 66208
        self.AR_COLS = 81908
        self.arena = sb("arena", [TILE, self.AR_COLS], BF16)
        self._views = {}

        def view(off, n, parts=TILE):
            ap = self.arena[0:parts, off:off + n]
            self._views[id(ap)] = off
            return ap

        self.qtm = view(self.A_QTM, D)
        self.ktm = view(self.A_KTM, D)
        self.qfmA_ = [view(self.A_QFMA, 2048, parts=64),
                      view(self.A_QFMA2, 2048, parts=64)]
        self.kfmA_ = [view(self.A_KFMA, 2048, parts=64),
                      view(self.A_KFMA2, 2048, parts=64)]
        self.vtA_ = [view(self.A_VTA, LA_H * 65), view(self.A_VTA2, LA_H * 65)]
        self.otm_ = [view(self.A_OTM, D), view(self.A_OTM2, D)]
        self.ofm = view(self.A_OFM, D)
        self.amsb = [view(self.A_AMSB, 512), view(self.A_AMSB + 512, 512)]
        self.ssb = view(self.A_SSB, 1024, parts=64)
        self.zsb = view(self.A_ZSB, 16, parts=64)
        self.pbf = [view(self.A_PBF, 640), view(self.A_PBF + 640, 640),
                    view(self.A_PBF2, 640), view(self.A_PBF2 + 640, 640)]
        self.ptsb = [view(self.A_PTSB, 640), view(self.A_PTSB + 640, 640),
                     view(self.A_PTSB2, 640), view(self.A_PTSB2 + 640, 640)]
        self.m01 = [view(self.A_M01, 640), view(self.A_M01 + 640, 640)]
        self.qfmB_ = [view(self.A_QFMB, 1024, parts=64),
                      view(self.A_QFMB2, 1024, parts=64)]
        self.qrotB_ = [view(self.A_QROTB, INNER), view(self.A_QROTB2, INNER)]
        self.krotB_ = [view(self.A_KROTB, INNER), view(self.A_KROTB2, INNER)]
        self.otmB_ = [view(self.A_OTMB, INNER), view(self.A_OTMB2, INNER)]
        self.ofmB_ = [view(self.A_OFMB, INNER), view(self.A_OFMB2, INNER)]
        self.glbf = view(self.A_GLBF, 512)
        self.hbrow = sb("hbrow", [1, 1024], F32)
        self.h1fm = [view(self.A_H1FM, D), view(self.A_H1FM + D, D)]
        self.lnjunk = view(self.A_LNJ, 2048)
        self.gfmB_ = [view(self.A_GFM, 512), view(self.A_LNJ + 512, 512)]
        self.glbf2 = view(self.A_LNJ, 512)
        self.hsx = [self.ar_f32(self.A_HSX, 1024), self.ar_f32(self.A_HSX + 2048, 1024)]
        self.qkvb_bf = view(self.A_QKVBR, 3 * D, parts=1)
        self.outb_bf = view(self.A_OUTBR, D, parts=1)
        self.b1_bf = view(self.A_B1R, H2, parts=1)
        self.b2_bf = view(self.A_B2R, D, parts=1)

    def _mk_psum(self):
        nc, es = self.nc, self.es
        ps = lambda n, s, dt: es.enter_context(nc.psum_tensor(n, s, dt))
        self.pb0 = ps("pb0", [TILE, 512], F32)
        self.pb1 = ps("pb1", [TILE, 512], F32)
        self.psA = ps("psA", [TILE, 1024], F32)
        self.psB = ps("psB", [TILE, 1024], F32)
        self.pt0 = ps("pt0", [TILE, 1024], BF16)
        self.pt1 = ps("pt1", [TILE, 1024], BF16)

    def r_bank(self, handle, bank):
        bb = 2048
        return (handle.name, bank * bb, (bank + 1) * bb)

    def r_all(self, obj):
        off = self._views.get(id(obj)) if hasattr(self, "_views") else None
        if off is not None:
            n = obj.shape[-1]
            return ("arena", off * 2, (off + n) * 2)
        return (obj.name, 0, 1 << 40)

    def ar(self, lo, n):
        return self.arena[:, lo:lo + n]

    def r_ar(self, lo, n):
        return ("arena", lo * 2, (lo + n) * 2)

    def ar_f32(self, lo_bf, n_f32):
        v = self.arena[:, :].bitcast(F32)
        return v[:, lo_bf // 2: lo_bf // 2 + n_f32]

    def build(self, stop_after=None):
        todo = [self.ph_init, self.ph_A,
                lambda: self.ph_Ba(0), lambda: self.ph_Bf(0),
                lambda: self.ph_Ba(1), lambda: self.ph_Bf(1), self.ph_C]
        names = ["init", "A", "B0a", "B0f", "B1a", "B1f", "C"]
        for nm, fn in zip(names, todo):
            fn()
            if stop_after is not None and nm == stop_after:
                break
        counters = self.P.finalize()
        sems = {}
        for k in sorted(counters):
            sems[k] = self.es.enter_context(self.nc.semaphore("s_" + k))
        self.P.emit(sems)
        return self.nc

    def ph_init(self):
        P, K = self.P, self
        P.phase("init")
        it = F32
        P.step("sync", writes=[K.r_all(K.cmask)],
               f=lambda e: e.dma_start(out=K.cmask[:, :], in_=K.i_cmask.ap()), dma=True)
        P.step("sync", writes=[K.r_all(K.ident)],
               f=lambda e: e.dma_start(out=K.ident[:, :], in_=K.i_ident.ap()), dma=True)
        P.step("sync", writes=[K.r_all(K.epsc)],
               f=lambda e: e.dma_start(out=K.epsc[:, :], in_=K.i_epsc.ap()), dma=True)
        P.step("vector", writes=[K.r_all(K.ones_row)],
               f=lambda e: e.memset(K.ones_row[:, :], 1.0))
        P.step("vector", writes=[K.r_all(K.zrow)],
               f=lambda e: e.memset(K.zrow[:, :], 0.0))
        P.step("vector", writes=[K.r_all(K.tin[0])],
               f=lambda e: e.memset(K.tin[0][:, :], 0.0))
        rowb = D * 4
        for buf in (K.d_x1, K.d_sa, K.d_sb):
            for t in range(10):
                P.step("gpsimd", reads=[K.r_all(K.tin[0])],
                       writes=[K.r_rows(buf, t * TILE, (t + 1) * TILE, rowb)],
                       f=lambda e, buf=buf, t=t: e.dma_start(
                           out=buf.ap()[t * TILE:(t + 1) * TILE, :], in_=K.tin[0][:, :]),
                       dma=True)
        P.step("gpsimd", writes=[K.r_ar(K.A_QKVW, 8 * 3072)],
               f=lambda e: e.dma_start(
                   out=K.ar(K.A_QKVW, 8 * 3072).rearrange("p (c n) -> p c n", c=8),
                   in_=K.i_qkvw.ap().rearrange("(c p) n -> p c n", p=TILE)), dma=True)
        P.step("gpsimd", writes=[K.r_ar(K.A_OUTW, 8 * 1024)],
               f=lambda e: e.dma_start(
                   out=K.ar(K.A_OUTW, 8 * 1024).rearrange("p (c n) -> p c n", c=8),
                   in_=K.i_outw.ap().rearrange("(c p) n -> p c n", p=TILE)), dma=True)
        P.step("gpsimd", writes=[K.r_all(K.qkvb_bf)],
               f=lambda e: e.dma_start(out=K.qkvb_bf[:, :], in_=K.i_qkvb.ap()), dma=True)
        P.step("gpsimd", writes=[K.r_all(K.outb_bf)],
               f=lambda e: e.dma_start(out=K.outb_bf[:, :], in_=K.i_outb.ap()), dma=True)

    def _ln_stats(self, src_reads, src_ap, n, rstd_col, nmr_col, mask_mul=None,
                  par=0):
        P, K = self.P, self
        stt = K.stt
        c_s1, c_ss, c_mu, c_var = 32 + 4 * par, 33 + 4 * par, 34 + 4 * par, 35 + 4 * par
        r_out = [K.r_cols(stt, rstd_col, rstd_col + 1, 4),
                 K.r_cols(stt, nmr_col, nmr_col + 1, 4)]
        r_tmp = [K.r_cols(stt, c_s1, c_s1 + 4, 4)]
        P.step("vector", reads=src_reads, writes=[r_tmp[0]],
               f=lambda e: e.reduce_sum(stt[:, c_s1:c_s1 + 1], src_ap, axis=AX.X))
        jo = (par % 2) * 1024
        P.step("scalar", reads=src_reads,
               writes=[r_tmp[0], K.r_cols(K.lnjunk, jo, jo + n, 2)],
               f=lambda e: e.activation(K.lnjunk[:, jo:jo + n],
                                        src_ap, AF.Square,
                                        accum_out=stt[:, c_ss:c_ss + 1]))
        def t1(e):
            e.tensor_scalar(stt[:, c_mu:c_mu + 1], stt[:, c_s1:c_s1 + 1],
                            1.0 / n, None, OP.mult)
            return e.tensor_scalar(stt[:, c_var:c_var + 1], stt[:, c_ss:c_ss + 1],
                                   1.0 / n, None, OP.mult)
        P.step("vector", reads=[r_tmp[0]], writes=[r_tmp[0]], f=t1)
        P.step("vector", reads=[r_tmp[0]], writes=[r_tmp[0]],
               f=lambda e: e.tensor_tensor(stt[:, c_s1:c_s1 + 1],
                                           stt[:, c_mu:c_mu + 1],
                                           stt[:, c_mu:c_mu + 1], OP.mult))
        P.step("vector", reads=[r_tmp[0]], writes=[r_tmp[0]],
               f=lambda e: e.tensor_tensor(stt[:, c_var:c_var + 1],
                                           stt[:, c_var:c_var + 1],
                                           stt[:, c_s1:c_s1 + 1], OP.subtract))
        P.step("scalar", reads=[r_tmp[0], K.r_all(K.epsc)], writes=[r_tmp[0]],
               f=lambda e: e.activation(stt[:, c_var:c_var + 1],
                                        stt[:, c_var:c_var + 1], AF.Ln,
                                        bias=K.epsc[:, 0:1]))
        P.step("scalar", reads=[r_tmp[0]], writes=[r_tmp[0], r_out[0]],
               f=lambda e: e.activation(stt[:, rstd_col:rstd_col + 1],
                                        stt[:, c_var:c_var + 1], AF.Exp, scale=-0.5))
        if mask_mul is not None:
            P.step("vector", reads=[r_out[0], mask_mul[1]], writes=[r_out[0]],
                   f=lambda e: e.tensor_tensor(
                       stt[:, rstd_col:rstd_col + 1], stt[:, rstd_col:rstd_col + 1],
                       mask_mul[0], OP.mult))
        P.step("vector", reads=[r_tmp[0], r_out[0]], writes=[r_out[1]],
               f=lambda e: e.scalar_tensor_tensor(
                   out=stt[:, nmr_col:nmr_col + 1], in0=stt[:, c_mu:c_mu + 1],
                   scalar=-1.0, in1=stt[:, rstd_col:rstd_col + 1],
                   op0=OP.mult, op1=OP.mult))
        return r_out

    def _transpose8(self, src, src_reads, pt, dst_ap, dst_writes, nch=8):
        P, K = self.P, self
        def tr(e):
            last = None
            for c in range(nch):
                last = e.transpose(pt[:, c * TILE:(c + 1) * TILE],
                                   src[:, c * TILE:(c + 1) * TILE], K.ident[:, :])
            return last
        P.step("tensor", reads=src_reads + [K.r_all(K.ident)],
               writes=[K.r_all(pt)], f=tr)
        P.step("vector", reads=[K.r_all(pt)], writes=dst_writes,
               f=lambda e: e.tensor_copy(dst_ap, pt[:, 0:nch * TILE]))

    def ph_A(self):
        P, K = self.P, self
        P.phase("A")
        rowb = D * 4
        qkvw = lambda c, lo, n: K.ar(K.A_QKVW + c * 3072 + lo, n)
        outw = lambda c, lo, n: K.ar(K.A_OUTW + c * 1024 + lo, n)
        r_qkvw = K.r_ar(K.A_QKVW, 8 * 3072)
        r_outw = K.r_ar(K.A_OUTW, 8 * 1024)
        for sbank in range(2):
            P.step("tensor", reads=[K.r_all(K.zrow)], writes=[K.r_bank(K.psA, sbank)],
                   f=lambda e, sbank=sbank: e.matmul(
                       K.psA[:, sbank * 512:(sbank + 1) * 512],
                       K.zrow[0:1, 0:TILE], K.zrow[0:1, 0:512],
                       start=True, stop=False, skip_group_check=True))
        P.step("tensor", reads=[K.r_all(K.zrow)], writes=[K.r_all(K.pb1)],
               f=lambda e: e.matmul(K.pb1[:, :], K.zrow[0:1, 0:TILE],
                                    K.zrow[0:1, 0:512],
                                    start=True, stop=False, skip_group_check=True))

        ALL_T = PRE_T + SPAN_T
        if self.a_limit is not None:
            ALL_T = min(ALL_T, self.a_limit)
        for t in range(ALL_T):
            span = t >= PRE_T
            tt = t - PRE_T
            buf = t % 2
            xin = K.tin[buf]
            r_xin = K.r_all(xin)
            row0 = t * TILE
            P.step("sync", writes=[r_xin], dma=True,
                   f=lambda e, row0=row0, xin=xin: e.dma_start(
                       out=xin[:, :], in_=K.i_x0.ap()[row0:row0 + TILE, :]))
            mr = K.mrow[buf]
            r_mr = K.r_all(mr)
            P.step("sync", writes=[r_mr], dma=True,
                   f=lambda e, row0=row0, mr=mr: e.dma_start(
                       out=mr[:, :], in_=K.i_rmask.ap()[row0:row0 + TILE, :]))
            rc = buf * 2
            self._ln_stats([r_xin], xin[:, :], D, rc, rc + 1, par=buf)
            h1 = K.tbf[buf]
            r_h1 = K.r_all(h1)
            P.step("scalar", reads=[r_xin, K.r_cols(K.stt, rc, rc + 2, 4)],
                   writes=[r_h1],
                   f=lambda e, xin=xin, h1=h1, rc=rc: e.activation(
                       h1[:, :], xin[:, :], AF.Identity,
                       scale=K.stt[:, rc:rc + 1], bias=K.stt[:, rc + 1:rc + 2]))
            hfm = K.h1fm[buf]
            r_hfm = K.r_all(hfm)
            self._transpose8(h1, [r_h1], K.pt0, hfm[:, :], [r_hfm])
            g0 = 0 if span else 2
            for g in range(g0, 6):
                pb = (K.pb0[:, :], K.psB[:, 0:512])[g % 2]
                r_pb = K.r_all(K.pb0) if g % 2 == 0 else K.r_bank(K.psB, 0)
                def mm(e, g=g, hfm=hfm, pb=pb):
                    for c in range(8):
                        e.matmul(pb, hfm[:, c * TILE:(c + 1) * TILE],
                                 qkvw(c, g * 512, 512), start=(c == 0), stop=False)
                    return e.matmul(pb, K.ones_row[:, :],
                                    K.qkvb_bf[:, g * 512:(g + 1) * 512],
                                    start=False, stop=True)
                P.step("tensor", reads=[r_hfm, r_qkvw, K.r_all(K.qkvb_bf),
                                        K.r_all(K.ones_row)],
                       writes=[r_pb], f=mm)
                if g < 2:
                    P.step("vector", reads=[r_pb], writes=[K.r_cols(K.qtm, g * 512, g * 512 + 512, 2)],
                           f=lambda e, g=g, pb=pb: e.tensor_scalar(
                               K.qtm[:, g * 512:(g + 1) * 512], pb,
                               0.0, 1e-6, OP.max, OP.add))
                elif g < 4:
                    gk = g - 2
                    r_qc = K.r_all(K.qc[gk])
                    P.step("vector", reads=[r_pb], writes=[r_qc],
                           f=lambda e, gk=gk, pb=pb: e.tensor_scalar(
                               K.qc[gk][:, :], pb, 0.0, 1e-6, OP.max, OP.add))
                    P.step("vector", reads=[r_qc, r_mr],
                           writes=[K.r_cols(K.ktm, gk * 512, gk * 512 + 512, 2)],
                           f=lambda e, gk=gk, mr=mr: e.tensor_tensor(
                               K.ktm[:, gk * 512:(gk + 1) * 512], K.qc[gk][:, :],
                               mr[:, 0:1].broadcast_to([TILE, 512]), OP.mult))
                else:
                    gv = g - 4
                    vtA = K.vtA_[buf]
                    P.step("vector", reads=[r_pb],
                           writes=[K.r_cols(vtA, gv * 8 * 65, (gv + 1) * 8 * 65, 2)],
                           f=lambda e, gv=gv, pb=pb, vtA=vtA: e.tensor_copy(
                               vtA[:, :].rearrange("p (h c) -> p h c", c=65)
                               [:, gv * 8:(gv + 1) * 8, 0:64],
                               pb.rearrange("p (h c) -> p h c", c=64)))
            if t in (0, 1):
                vtA = K.vtA_[buf]
                P.step("vector", writes=[K.r_all(vtA)], tag="vones",
                       f=lambda e, vtA=vtA: e.memset(
                           vtA[:, :].rearrange("p (h c) -> p h c", c=65)[:, :, 64:65], 1.0))

            r_ssb = K.r_all(K.ssb)
            if span and not (K.a_parts & 1):
                span = False
            if span:
                r_zsb = K.r_all(K.zsb)
                P.step("vector", reads=[K.r_bank(K.psA, 0), K.r_bank(K.psA, 1)],
                       writes=[r_ssb],
                       f=lambda e: e.tensor_copy(K.ssb[0:64, :], K.psA[0:64, :]))
                P.step("vector", reads=[K.r_all(K.pb1)], writes=[r_zsb],
                       f=lambda e: e.tensor_copy(K.zsb[0:64, :], K.pb1[0:64, 0:16]))
                kfmA, qfmA = K.kfmA_[buf], K.qfmA_[buf]
                otm = K.otm_[buf]
                for src_tm, dst_fm, r_dst in ((K.ktm, kfmA, K.r_all(kfmA)),
                                              (K.qtm, qfmA, K.r_all(qfmA))):
                    for half in range(2):
                        def trh(e, src_tm=src_tm, half=half):
                            last = None
                            for j in range(8):
                                h = half * 8 + j
                                last = e.transpose(
                                    K.pt0[0:64, j * TILE:(j + 1) * TILE],
                                    src_tm[:, h * 64:(h + 1) * 64], K.ident[:, :])
                            return last
                        P.step("tensor", reads=[K.r_all(src_tm), K.r_all(K.ident)],
                               writes=[K.r_all(K.pt0)], f=trh)
                        P.step("vector", reads=[K.r_all(K.pt0)], writes=[r_dst],
                               f=lambda e, dst_fm=dst_fm, half=half: e.tensor_copy(
                                   dst_fm[0:64, half * 1024:(half + 1) * 1024],
                                   K.pt0[0:64, 0:1024]))
                for grp in (range(4) if (K.a_parts & 4) else []):
                    r_pbank = K.r_bank(K.psB, 1)
                    def at(e, grp=grp, kfmA=kfmA, qfmA=qfmA):
                        last = None
                        for j in range(4):
                            h = grp * 4 + j
                            last = e.matmul(
                                K.psB[:, 512 + j * TILE: 512 + (j + 1) * TILE],
                                kfmA[0:64, h * TILE:(h + 1) * TILE],
                                qfmA[0:64, h * TILE:(h + 1) * TILE],
                                start=True, stop=True)
                        return last
                    P.step("tensor", reads=[K.r_all(qfmA), K.r_all(kfmA)],
                           writes=[r_pbank], f=at)
                    am = K.amsb[grp % 2]
                    P.step("vector", reads=[r_pbank, K.r_all(K.cmask)],
                           writes=[K.r_all(am)],
                           f=lambda e, am=am: e.tensor_tensor(
                               am[:, :].rearrange("p (a b) -> p a b", a=4),
                               K.psB[:, 512:1024]
                               .rearrange("p (a b) -> p a b", a=4),
                               K.cmask[:, :].unsqueeze(1).broadcast_to([TILE, 4, TILE]),
                               OP.mult))
                    if not (K.a_parts & 8):
                        continue
                    nd = (K.pb0[:, :], K.psB[:, 0:512])[grp % 2]
                    r_nd = K.r_all(K.pb0) if grp % 2 == 0 else K.r_bank(K.psB, 0)
                    oc0 = 8 + 8 * (grp % 2)
                    def ndm(e, grp=grp, am=am, nd=nd, qfmA=qfmA, vtA=K.vtA_[buf]):
                        last = None
                        for j in range(4):
                            h = grp * 4 + j
                            e.matmul(nd[:, j * 65:(j + 1) * 65],
                                     am[:, j * TILE:(j + 1) * TILE],
                                     vtA[:, h * 65:(h + 1) * 65],
                                     start=True, stop=False)
                            e.matmul(nd[:, j * 65:j * 65 + 64],
                                     qfmA[0:64, h * TILE:(h + 1) * TILE],
                                     K.ssb[0:64, h * 64:(h + 1) * 64],
                                     start=False, stop=False)
                            last = e.matmul(nd[:, j * 65 + 64:j * 65 + 65],
                                            qfmA[0:64, h * TILE:(h + 1) * TILE],
                                            K.zsb[0:64, h:h + 1],
                                            start=False, stop=True)
                        return last
                    P.step("tensor", reads=[K.r_all(am), K.r_all(K.vtA_[buf]),
                                            K.r_all(qfmA), r_ssb, r_zsb],
                           writes=[r_nd], f=ndm)
                    P.step("vector", reads=[r_nd],
                           writes=[K.r_cols(K.stt, oc0, oc0 + 4, 4)],
                           f=lambda e, nd=nd, oc0=oc0: e.tensor_scalar(
                               K.stt[:, oc0:oc0 + 4],
                               nd[:, 0:260].rearrange("p (h c) -> p h c", c=65)
                               [:, :, 64:65].squeeze(2),
                               1e-6, None, OP.max))
                    P.step("vector", reads=[K.r_cols(K.stt, oc0, oc0 + 4, 4)],
                           writes=[K.r_cols(K.stt, oc0 + 4, oc0 + 8, 4)],
                           f=lambda e, oc0=oc0: e.reciprocal(
                               K.stt[:, oc0 + 4:oc0 + 8], K.stt[:, oc0:oc0 + 4]))
                    P.step("vector", reads=[r_nd, K.r_cols(K.stt, oc0 + 4, oc0 + 8, 4)],
                           writes=[K.r_cols(otm, grp * 256, (grp + 1) * 256, 2)],
                           f=lambda e, grp=grp, nd=nd, oc0=oc0, otm=otm: e.tensor_tensor(
                               otm[:, grp * 256:(grp + 1) * 256]
                               .rearrange("p (h c) -> p h c", c=64),
                               nd[:, 0:260].rearrange("p (h c) -> p h c", c=65)[:, :, 0:64],
                               K.stt[:, oc0 + 4:oc0 + 8].unsqueeze(2)
                               .broadcast_to([TILE, 4, 64]),
                               OP.mult))
            if K.a_parts & 16:
                def su(e, vtA=K.vtA_[buf]):
                    last = None
                    for h in range(LA_H):
                        e.matmul(K.psA[0:64, h * 64:(h + 1) * 64],
                                 K.ktm[:, h * 64:(h + 1) * 64],
                                 vtA[:, h * 65:h * 65 + 64],
                                 start=False, stop=False, skip_group_check=True)
                        last = e.matmul(K.pb1[0:64, h:h + 1],
                                        K.ktm[:, h * 64:(h + 1) * 64],
                                        vtA[:, h * 65 + 64:h * 65 + 65],
                                        start=False, stop=False, skip_group_check=True)
                    return last
                P.step("tensor", reads=[K.r_all(K.ktm), K.r_all(K.vtA_[buf])],
                       writes=[K.r_bank(K.psA, 0), K.r_bank(K.psA, 1),
                               K.r_all(K.pb1)], f=su)
            if span and not (K.a_parts & 32):
                span = False
            if span:
                self._transpose8(otm, [K.r_cols(otm, 0, 1024, 2)], K.pt0,
                                 K.ofm[:, :], [K.r_all(K.ofm)])
                for g in range(2):
                    bank = g
                    def om(e, g=g):
                        for c in range(8):
                            e.matmul(K.psB[:, g * 512:(g + 1) * 512],
                                     K.ofm[:, c * TILE:(c + 1) * TILE],
                                     outw(c, g * 512, 512), start=(c == 0), stop=False)
                        return e.matmul(K.psB[:, g * 512:(g + 1) * 512],
                                        K.ones_row[:, :],
                                        K.outb_bf[:, g * 512:(g + 1) * 512],
                                        start=False, stop=True)
                    P.step("tensor", reads=[K.r_all(K.ofm), r_outw,
                                            K.r_all(K.outb_bf), K.r_all(K.ones_row)],
                           writes=[K.r_bank(K.psB, g)], f=om)
                if K.dbg:
                    P.step("gpsimd", reads=[K.r_all(K.otm_[buf])], dma=True,
                           writes=[("dbgbuf", 1, 2)],
                           f=lambda e, tt=tt, otm=K.otm_[buf]: e.dma_start(
                               out=K.d_dbg.ap()[tt * TILE:(tt + 1) * TILE,
                                                3 * D + 1040:4 * D + 1040],
                               in_=otm[:, :]))
                aout = K.taux[0]
                r_aout = K.r_all(aout)
                P.step("scalar", reads=[K.r_bank(K.psB, 0), K.r_bank(K.psB, 1)],
                       writes=[r_aout],
                       f=lambda e, aout=aout: e.copy(aout[:, :], K.psB[:, :]))
                ra = 4 + 2 * buf
                self._ln_stats([r_aout], aout[:, :], D, ra, ra + 1,
                               mask_mul=(mr[:, 0:1], r_mr), par=buf)
                lna = K.taux[1]
                r_lna = K.r_all(lna)
                P.step("scalar", reads=[r_aout, K.r_cols(K.stt, ra, ra + 2, 4)],
                       writes=[r_lna],
                       f=lambda e, aout=aout, lna=lna, ra=ra: e.activation(
                           lna[:, :], aout[:, :], AF.Identity,
                           scale=K.stt[:, ra:ra + 1], bias=K.stt[:, ra + 1:ra + 2]))
                x1t = xin
                P.step("vector", reads=[r_lna, r_xin], writes=[r_xin],
                       f=lambda e, x1t=x1t, lna=lna: e.tensor_tensor(
                           x1t[:, :], x1t[:, :], lna[:, :], OP.add))
                P.step("gpsimd", reads=[r_xin], dma=True,
                       writes=[K.r_rows(K.d_x1, tt * TILE, (tt + 1) * TILE, rowb)],
                       f=lambda e, tt=tt, x1t=x1t: e.dma_start(
                           out=K.d_x1.ap()[tt * TILE:(tt + 1) * TILE, :], in_=x1t[:, :]))
        for sbank in range(2):
            P.step("tensor", reads=[K.r_all(K.zrow)], writes=[K.r_bank(K.psA, sbank)],
                   f=lambda e, sbank=sbank: e.matmul(
                       K.psA[:, sbank * 512:(sbank + 1) * 512],
                       K.zrow[0:1, 0:TILE], K.zrow[0:1, 0:512],
                       start=False, stop=True, skip_group_check=True))
        P.step("tensor", reads=[K.r_all(K.zrow)], writes=[K.r_all(K.pb1)],
               f=lambda e: e.matmul(K.pb1[:, :], K.zrow[0:1, 0:TILE],
                                    K.zrow[0:1, 0:512],
                                    start=False, stop=True, skip_group_check=True))

    def ph_Ba(self, l):
        P, K = self.P, self
        P.phase(f"B{l}a")
        rowb = D * 4
        src = K.d_x1 if l == 0 else K.d_sb
        dst = K.d_sa
        kv_tiles = B0A_KV if l == 0 else B1A_KV
        full_tiles = B0A_FULL if l == 0 else B1A_FULL
        tiles = list(kv_tiles) + list(full_tiles)
        r_wqkv = K.r_ar(K.A_WQKV, 8 * 1536)
        r_wo = K.r_ar(K.A_WO, 4 * 1024)
        wqkv = lambda c, lo, n: K.ar(K.A_WQKV + c * 1536 + lo, n)
        wo_w = lambda c, lo, n: K.ar(K.A_WO + c * 1024 + lo, n)
        P.step("gpsimd", writes=[r_wqkv], dma=True,
               f=lambda e: e.dma_start(
                   out=K.ar(K.A_WQKV, 8 * 1536).rearrange("p (c n) -> p c n", c=8),
                   in_=K.i_wqkv.ap()[l].rearrange("(c p) n -> p c n", p=TILE)))
        P.step("gpsimd", writes=[K.r_ar(K.A_W2, 22 * 1024)], dma=True,
               f=lambda e: e.dma_start(
                   out=K.ar(K.A_W2, 21 * 1024).rearrange("p (c n) -> p c n", c=21),
                   in_=K.i_w2.ap()[l][0:2688].rearrange("(c p) n -> p c n", p=TILE)))
        P.step("gpsimd", writes=[K.r_ar(K.A_W2, 22 * 1024)], dma=True,
               f=lambda e: e.dma_start(
                   out=K.ar(K.A_W2 + 21 * 1024, 1024)[0:42, :],
                   in_=K.i_w2.ap()[l][2688:2730]))
        P.step("gpsimd", writes=[K.r_ar(K.A_W1 + 7 * H2, H2)], dma=True,
               f=lambda e: e.dma_start(
                   out=K.ar(K.A_W1 + 7 * H2, H2),
                   in_=K.i_w1.ap()[l][7 * TILE:8 * TILE]))
        P.step("gpsimd", writes=[r_wo], dma=True,
               f=lambda e: e.dma_start(
                   out=K.ar(K.A_WO, 4 * 1024).rearrange("p (c n) -> p c n", c=4),
                   in_=K.i_wo.ap()[l].rearrange("(c p) n -> p c n", p=TILE)))
        P.step("sync", writes=[K.r_all(K.scl)], dma=True,
               f=lambda e: e.dma_start(out=K.scl[:, :], in_=K.i_scl.ap()[l]))
        r_lng = ("arena", K.A_LNG * 2, K.A_LNG * 2 + 1024 * 4)
        r_lnb = ("arena", K.A_LNG * 2 + 1024 * 4, K.A_LNG * 2 + 2048 * 4)
        P.step("sync", writes=[r_lng, r_lnb], dma=True,
               f=lambda e: e.dma_start(out=K.ar_f32(K.A_LNG, 2048),
                                       in_=K.i_lnrep.ap()[l]))
        lng = K.ar_f32(K.A_LNG, 2048)[:, 0:1024]
        lnb = K.ar_f32(K.A_LNG, 2048)[:, 1024:2048]
        P.step("vector", writes=[K.r_ar(K.A_XFM, 2 * 8 * 129)],
               f=lambda e: e.memset(K.ar(K.A_XFM, 2 * 8 * 129), 0.0))

        first = True
        pend_heads = None
        pend_fold1 = []
        pend_fold2 = []
        for tt in tiles:
            P.begin_capture()
            full = tt in full_tiles
            buf = tt % 2
            xin = K.tin[buf]
            r_xin = K.r_all(xin)
            row0 = tt * TILE
            P.step("sync", writes=[r_xin], dma=True,
                   f=lambda e, row0=row0, xin=xin: e.dma_start(
                       out=xin[:, :], in_=src.ap()[row0:row0 + TILE, :]))
            rtb = K.rt[buf]
            r_rtb = K.r_all(rtb)
            P.step("sync", writes=[r_rtb], dma=True,
                   f=lambda e, row0=row0, rtb=rtb: e.dma_start(
                       out=rtb[:, :], in_=K.i_rot.ap()[row0:row0 + TILE, :]))
            if l == 0:
                rc = buf * 2
                self._ln_stats([r_xin], xin[:, :], D, rc, rc + 1, par=buf)
                h2 = K.taux[buf]
                r_h2 = K.r_all(h2)
                P.step("scalar", reads=[r_xin, K.r_cols(K.stt, rc, rc + 2, 4)],
                       writes=[r_h2],
                       f=lambda e, xin=xin, h2=h2, rc=rc: e.activation(
                           h2[:, :], xin[:, :], AF.Identity,
                           scale=K.stt[:, rc:rc + 1], bias=K.stt[:, rc + 1:rc + 2]))
            else:
                h2 = xin
                r_h2 = r_xin
            hbf = K.tbf[buf]
            r_hbf = K.r_all(hbf)
            P.step("vector", reads=[r_h2], writes=[r_hbf],
                   f=lambda e, h2=h2, hbf=hbf: e.tensor_copy(hbf[:, :], h2[:, :]))
            slot = tt % 2
            xo = K.A_XFM + slot * 8 * 129
            xo_prev = K.A_XFM + (1 - slot) * 8 * 129
            r_xfm = K.r_ar(xo, 8 * 129)
            xfm3 = K.ar(xo, 8 * 129).rearrange("p (c n) -> p c n", c=8)
            if not first:
                P.step("vector", reads=[K.r_ar(xo_prev, 8 * 129)], writes=[r_xfm],
                       f=lambda e, xo=xo, xo_prev=xo_prev: e.tensor_copy(
                           K.ar(xo, 8 * 129).rearrange("p (c n) -> p c n", c=8)[:, :, 0:1],
                           K.ar(xo_prev, 8 * 129).rearrange("p (c n) -> p c n", c=8)[:, :, 128:129]))
            else:
                P.step("vector", writes=[r_xfm],
                       f=lambda e, xo=xo: e.memset(
                           K.ar(xo, 8 * 129).rearrange("p (c n) -> p c n", c=8)[:, :, 0:1], 0.0))
            def tr(e, hbf=hbf, xfm3=xfm3):
                last = None
                for c in range(8):
                    last = e.transpose(K.pt0[:, c * TILE:(c + 1) * TILE],
                                       hbf[:, c * TILE:(c + 1) * TILE], K.ident[:, :])
                return last
            P.step("tensor", reads=[r_hbf, K.r_all(K.ident)], writes=[K.r_all(K.pt0)], f=tr)
            P.step("vector", reads=[K.r_all(K.pt0)], writes=[r_xfm],
                   f=lambda e, xfm3=xfm3: e.tensor_copy(
                       xfm3[:, :, 1:129],
                       K.pt0[:, :].rearrange("p (c n) -> p c n", c=8)))
            for g in range(3):
                pb = (K.pb0, K.pb1)[g % 2]
                r_pb = K.r_all(pb)
                def mm(e, g=g, xo=xo, pb=pb):
                    x3 = K.ar(xo, 8 * 129).rearrange("p (c n) -> p c n", c=8)
                    last = None
                    for c in range(8):
                        lhs = x3[:, c, 1:129] if c < 4 else x3[:, c, 0:128]
                        last = e.matmul(pb[:, :], lhs, wqkv(c, g * 512, 512),
                                        start=(c == 0), stop=(c == 7))
                    return last
                P.step("tensor", reads=[r_xfm, r_wqkv], writes=[r_pb], f=mm)
                if K.dbg and g == 0 and full:
                    P.step("vector", reads=[r_pb], writes=[K.r_all(K.gel)],
                           f=lambda e, pb=pb: e.tensor_copy(K.gel[:, :], pb[:, :]))
                    P.step("gpsimd", reads=[K.r_all(K.gel)], dma=True,
                           writes=[("dbgbuf", 1, 2)],
                           f=lambda e, tt=tt: e.dma_start(
                               out=K.d_dbg.ap()[tt * TILE:(tt + 1) * TILE, 3880:4904]
                               .bitcast(mybir.dt.float32),
                               in_=K.gel[:, :]))
                if g == 2:
                    vs = K.A_VSP + (tt % 6) * 520
                    r_vs = ("vsp_slot", tt % 6, tt % 6 + 1)
                    P.step("vector", reads=[r_pb], writes=[r_vs],
                           f=lambda e, vs=vs, pb=pb: e.tensor_copy(
                               K.ar(vs, 520).rearrange("p (h c) -> p h c", c=65)[:, :, 0:64],
                               pb[:, :].rearrange("p (h c) -> p h c", c=64)))
                    P.step("vector", reads=[], writes=[r_vs], tag="vone",
                           f=lambda e, vs=vs: e.memset(
                               K.ar(vs, 520).rearrange("p (h c) -> p h c", c=65)[:, :, 64:65], 1.0))
                else:
                    sc = (8 if g == 0 else 16) + 32 * buf
                    gel = (K.gel, K.gel2)[buf]
                    r_gel = K.r_all(gel)
                    r_ss = K.r_cols(K.stt, sc, sc + 8, 4)
                    P.step("scalar", reads=[r_pb], writes=[r_gel],
                           f=lambda e, pb=pb, gel=gel: e.activation(
                               gel[:, :], pb[:, :], AF.Square))
                    P.step("vector", reads=[r_gel], writes=[r_ss],
                           f=lambda e, sc=sc, gel=gel: e.reduce_sum(
                               K.stt[:, sc:sc + 8],
                               gel[:, :].rearrange("p (h c) -> p h c", c=64),
                               axis=AX.X))
                    if K.dbg and g == 0 and full:
                        P.step("gpsimd", reads=[r_ss], dma=True,
                               writes=[("dbgbuf", 1, 2)],
                               f=lambda e, tt=tt, sc=sc: e.dma_start(
                                   out=K.d_dbg.ap()[tt * TILE:(tt + 1) * TILE,
                                                    4904:4920]
                                   .bitcast(mybir.dt.float32),
                                   in_=K.stt[:, sc:sc + 8]))
                    P.step("vector", reads=[r_ss], writes=[r_ss],
                           f=lambda e, sc=sc: e.tensor_scalar(
                               K.stt[:, sc:sc + 8], K.stt[:, sc:sc + 8],
                               1e-24, None, OP.max))
                    P.step("scalar", reads=[r_ss], writes=[r_ss],
                           f=lambda e, sc=sc: e.activation(
                               K.stt[:, sc:sc + 8], K.stt[:, sc:sc + 8], AF.Ln))
                    P.step("scalar", reads=[r_ss], writes=[r_ss],
                           f=lambda e, sc=sc: e.activation(
                               K.stt[:, sc:sc + 8], K.stt[:, sc:sc + 8],
                               AF.Exp, scale=-0.5))
                    so = 0 if g == 0 else 128
                    ro = 0 if g == 0 else 128
                    po = 0 if g == 0 else 128
                    qfac = QK_SCALE if g == 0 else 1.0
                    rpr = (K.rprep, K.rprep2)[buf]
                    r_rp = K.r_cols(rpr, po, po + 128, 4)
                    P.step("vector", reads=[r_rtb, K.r_all(K.scl)], writes=[r_rp],
                           f=lambda e, so=so, ro=ro, po=po, qfac=qfac, rtb=rtb, rpr=rpr:
                           e.scalar_tensor_tensor(
                               out=rpr[:, po:po + 64], in0=rtb[:, ro:ro + 64],
                               scalar=qfac, in1=K.scl[:, so:so + 64],
                               op0=OP.mult, op1=OP.mult))
                    P.step("vector", reads=[r_rtb, K.r_all(K.scl)], writes=[r_rp],
                           f=lambda e, so=so, ro=ro, po=po, qfac=qfac, rtb=rtb, rpr=rpr:
                           e.scalar_tensor_tensor(
                               out=rpr[:, po + 64:po + 128], in0=rtb[:, ro + 64:ro + 128],
                               scalar=qfac, in1=K.scl[:, so + 64:so + 128],
                               op0=OP.mult, op1=OP.mult))
                    qa, qb, qcv = (K.qc[0], K.qc[1], K.qc[2]) if buf == 0 else \
                        (K.qc[3], K.qc[4], K.qc[5])
                    r_qa, r_qb, r_qc2 = K.r_all(qa), K.r_all(qb), K.r_all(qcv)
                    P.step("scalar", reads=[r_pb], writes=[r_qa],
                           f=lambda e, pb=pb, qa=qa: e.copy(qa[:, :], pb[:, :]))
                    P.step("vector", reads=[r_qa, r_rp], writes=[r_qb],
                           f=lambda e, po=po, qa=qa, qb=qb, rpr=rpr: e.tensor_tensor(
                               qb[:, :].rearrange("p (h c) -> p h c", c=64),
                               qa[:, :].rearrange("p (h c) -> p h c", c=64),
                               rpr[:, po:po + 64].unsqueeze(1)
                               .broadcast_to([TILE, 8, 64]),
                               OP.mult))
                    def rot(e, qa=qa, qcv=qcv):
                        p3 = qa[:, :].rearrange("p (h c) -> p h c", c=64)
                        h3 = qcv[:, :].rearrange("p (h c) -> p h c", c=64)
                        e.tensor_scalar(h3[:, :, 0:32], p3[:, :, 32:64], -1.0, None, OP.mult)
                        return e.tensor_copy(h3[:, :, 32:64], p3[:, :, 0:32])
                    P.step("vector", reads=[r_qa], writes=[r_qc2], f=rot)
                    P.step("vector", reads=[r_qc2, r_rp], writes=[r_qa],
                           f=lambda e, po=po, qcv=qcv, qa=qa, rpr=rpr: e.tensor_tensor(
                               qa[:, :].rearrange("p (h c) -> p h c", c=64),
                               qcv[:, :].rearrange("p (h c) -> p h c", c=64),
                               rpr[:, po + 64:po + 128].unsqueeze(1)
                               .broadcast_to([TILE, 8, 64]),
                               OP.mult))
                    P.step("vector", reads=[r_qa, r_qb], writes=[r_qb],
                           f=lambda e, qa=qa, qb=qb: e.tensor_tensor(
                               qb[:, :], qb[:, :], qa[:, :], OP.add))
                    dst_rot = K.qrotB_[buf] if g == 0 else K.krotB_[buf]
                    r_drot = K.r_all(dst_rot)
                    P.step("vector", reads=[r_qb, r_ss], writes=[r_drot],
                           f=lambda e, sc=sc, qb=qb, dst_rot=dst_rot: e.tensor_tensor(
                               dst_rot[:, :].rearrange("p (h c) -> p h c", c=64),
                               qb[:, :].rearrange("p (h c) -> p h c", c=64),
                               K.stt[:, sc:sc + 8].unsqueeze(2)
                               .broadcast_to([TILE, 8, 64]),
                               OP.mult))
            slot6 = tt % 6
            def trk(e, krotB=K.krotB_[buf]):
                last = None
                for h in range(NH):
                    last = e.transpose(K.pt1[0:64, h * TILE:(h + 1) * TILE],
                                       krotB[:, h * 64:(h + 1) * 64], K.ident[:, :])
                return last
            P.step("tensor", reads=[K.r_all(K.krotB_[buf]), K.r_all(K.ident)],
                   writes=[K.r_all(K.pt1)], f=trk)
            r_kslot = ("kfm_slot", slot6, slot6 + 1)
            P.step("vector", reads=[K.r_all(K.pt1)], writes=[r_kslot],
                   f=lambda e, slot6=slot6: e.tensor_copy(
                       K.arena[0:64, K.A_KFM:K.A_KFM + 8 * 768]
                       .rearrange("p (g n) -> p g n", g=8)
                       [:, :, slot6 * TILE:(slot6 + 1) * TILE],
                       K.pt1[0:64, 0:1024].rearrange("p (c n) -> p c n", c=8)))
            if full:
                def trq(e, qrotB=K.qrotB_[buf]):
                    last = None
                    for h in range(NH):
                        last = e.transpose(K.pt0[0:64, h * TILE:(h + 1) * TILE],
                                           qrotB[:, h * 64:(h + 1) * 64],
                                           K.ident[:, :])
                    return last
                P.step("tensor", reads=[K.r_all(K.qrotB_[buf]), K.r_all(K.ident)],
                       writes=[K.r_all(K.pt0)], f=trq)
                P.step("vector", reads=[K.r_all(K.pt0)],
                       writes=[K.r_all(K.qfmB_[buf])],
                       f=lambda e, qfmB=K.qfmB_[buf]: e.tensor_copy(
                           qfmB[0:64, :], K.pt0[0:64, 0:1024]))
                m01 = K.m01[buf]
                r_m01 = K.r_all(m01)
                P.step("sync", writes=[r_m01], dma=True,
                       f=lambda e, tt=tt, m01=m01: e.dma_start(
                           out=m01[:, :], in_=K.i_m01.ap()[tt]))
                P.mark_split()
                qfmB = K.qfmB_[buf]
                def emit_sim(h, tt=tt, qfmB=qfmB):
                    psim = (K.psA, K.psB)[h % 2]
                    r_ps0, r_ps1 = K.r_bank(psim, 0), K.r_bank(psim, 1)
                    def sim(e, h=h, psim=psim, tt=tt, qfmB=qfmB):
                        segs = []
                        i = 0
                        while i < 5:
                            s0 = (tt - 4 + i) % 6
                            n = 1
                            while i + n < 5 and s0 + n <= 5:
                                n += 1
                            segs.append((i, s0, n))
                            i += n
                        final = []
                        for (i0, s0, n) in segs:
                            if i0 * TILE < 512 < (i0 + n) * TILE:
                                k = (512 - i0 * TILE) // TILE
                                final.append((i0, s0, k))
                                final.append((i0 + k, s0 + k, n - k))
                            else:
                                final.append((i0, s0, n))
                        last = None
                        for (i0, s0, n) in final:
                            rhs = K.arena[0:64,
                                          K.A_KFM + h * 768 + s0 * TILE:
                                          K.A_KFM + h * 768 + (s0 + n) * TILE]
                            last = e.matmul(psim[:, i0 * TILE:(i0 + n) * TILE],
                                            qfmB[0:64, h * TILE:(h + 1) * TILE],
                                            rhs, start=True, stop=True)
                        return last
                    kslots = [("kfm_slot", (tt - 4 + i) % 6, (tt - 4 + i) % 6 + 1)
                              for i in range(5)]
                    P.step("tensor", reads=[K.r_all(qfmB)] + kslots,
                           writes=[r_ps0, r_ps1], f=sim)

                def emit_front(h, m01=m01, r_m01=r_m01, buf=buf):
                    psim = (K.psA, K.psB)[h % 2]
                    r_ps0, r_ps1 = K.r_bank(psim, 0), K.r_bank(psim, 1)
                    hp = 24 + 4 * (h % 2)
                    P.step("vector", reads=[r_ps0, r_ps1],
                           writes=[K.r_cols(K.stt, hp, hp + 1, 4)],
                           f=lambda e, psim=psim, hp=hp: e.reduce_max(
                               K.stt[:, hp:hp + 1], psim[:, 0:640],
                               axis=AX.X, negate=True))
                    pb_ = K.pbf[h % 2]
                    r_pb_ = K.r_all(pb_)
                    P.step("scalar", reads=[r_ps0, r_ps1, K.r_cols(K.stt, hp, hp + 1, 4)],
                           writes=[r_pb_],
                           f=lambda e, psim=psim, pb_=pb_, hp=hp: e.activation(
                               pb_[:, :], psim[:, 0:640], AF.Exp,
                               bias=K.stt[:, hp:hp + 1]))
                    p2 = K.ptsb[h % 2]
                    r_p2 = K.r_all(p2)
                    P.step("vector", reads=[r_pb_, r_m01], writes=[r_p2],
                           f=lambda e, pb_=pb_, p2=p2, m01=m01: e.tensor_tensor(
                               p2[:, :], pb_[:, :], m01[:, :], OP.mult))

                def emit_back(h, tt=tt, buf=buf):
                    hp = 24 + 4 * (h % 2)
                    pb_ = K.pbf[h % 2]
                    r_pb_ = K.r_all(pb_)
                    p2 = K.ptsb[h % 2]
                    r_p2 = K.r_all(p2)
                    pt = (K.pt0, K.pt1)[h % 2]
                    def trp(e, p2=p2, pt=pt):
                        last = None
                        for c in range(5):
                            last = e.transpose(pt[:, c * TILE:(c + 1) * TILE],
                                               p2[:, c * TILE:(c + 1) * TILE],
                                               K.ident[:, :])
                        return last
                    P.step("tensor", reads=[r_p2, K.r_all(K.ident)],
                           writes=[K.r_all(pt)], f=trp)
                    P.step("scalar", reads=[K.r_all(pt)], writes=[r_pb_],
                           f=lambda e, pb_=pb_, pt=pt: e.copy(pb_[:, 0:640],
                                                              pt[:, 0:640]))
                    psim = (K.psA, K.psB)[h % 2]
                    nd = psim[:, 768:833]
                    r_nd = K.r_bank(psim, 1)
                    def av(e, h=h, pb_=pb_, nd=nd, tt=tt):
                        last = None
                        for i in range(5):
                            vs = K.A_VSP + ((tt - 4 + i) % 6) * 520
                            last = e.matmul(nd[:, 0:65],
                                            pb_[:, i * TILE:(i + 1) * TILE],
                                            K.ar(vs, 520)[:, h * 65:(h + 1) * 65],
                                            start=(i == 0), stop=(i == 4))
                        return last
                    vslots = [("vsp_slot", (tt - 4 + i) % 6, (tt - 4 + i) % 6 + 1)
                              for i in range(5)]
                    P.step("tensor", reads=[r_pb_] + vslots,
                           writes=[r_nd], f=av)
                    P.step("vector", reads=[r_nd],
                           writes=[K.r_cols(K.stt, hp + 1, hp + 2, 4)],
                           f=lambda e, nd=nd, hp=hp: e.tensor_scalar(
                               K.stt[:, hp + 1:hp + 2], nd[:, 64:65],
                               1e-30, None, OP.max))
                    P.step("vector", reads=[K.r_cols(K.stt, hp + 1, hp + 2, 4)],
                           writes=[K.r_cols(K.stt, hp + 2, hp + 3, 4)],
                           f=lambda e, hp=hp: e.reciprocal(
                               K.stt[:, hp + 2:hp + 3], K.stt[:, hp + 1:hp + 2]))
                    P.step("vector", reads=[r_nd, K.r_cols(K.stt, hp + 2, hp + 3, 4)],
                           writes=[K.r_cols(K.otmB_[buf], h * 64, (h + 1) * 64, 2)],
                           f=lambda e, h=h, nd=nd, hp=hp, otmB=K.otmB_[buf]:
                           e.tensor_tensor(
                               otmB[:, h * 64:(h + 1) * 64], nd[:, 0:64],
                               K.stt[:, hp + 2:hp + 3].broadcast_to([TILE, 64]),
                               OP.mult))

                emit_sim(0)
                emit_sim(1)
                for h in range(NH):
                    emit_front(h)
                    if h + 2 < NH:
                        emit_sim(h + 2)
                    emit_back(h)
                def tro(e, otmB=K.otmB_[buf]):
                    last = None
                    for c in range(4):
                        last = e.transpose(K.pt1[:, c * TILE:(c + 1) * TILE],
                                           otmB[:, c * TILE:(c + 1) * TILE],
                                           K.ident[:, :])
                    return last
                P.step("tensor", reads=[K.r_all(K.otmB_[buf]), K.r_all(K.ident)],
                       writes=[K.r_all(K.pt1)], f=tro)
                P.step("vector", reads=[K.r_all(K.pt1)],
                       writes=[K.r_all(K.ofmB_[buf])],
                       f=lambda e, ofmB=K.ofmB_[buf]: e.tensor_copy(
                           ofmB[:, :], K.pt1[:, 0:512]))
                for g in range(2):
                    pb = (K.pb0, K.pb1)[g]
                    r_pb = K.r_all(pb)
                    def womm(e, g=g, pb=pb, ofmB=K.ofmB_[buf]):
                        last = None
                        for c in range(4):
                            last = e.matmul(pb[:, :], ofmB[:, c * TILE:(c + 1) * TILE],
                                            wo_w(c, g * 512, 512),
                                            start=(c == 0), stop=(c == 3))
                        return last
                    P.step("tensor", reads=[K.r_all(K.ofmB_[buf]), r_wo],
                           writes=[r_pb], f=womm)
                    P.step("vector", reads=[r_pb, r_h2], writes=[r_h2],
                           f=lambda e, g=g, pb=pb, h2=h2: e.tensor_tensor(
                               h2[:, g * 512:(g + 1) * 512], h2[:, g * 512:(g + 1) * 512],
                               pb[:, :], OP.add))
                P.step("sync", reads=[r_h2],
                       writes=[("hbrow", buf * 2048, buf * 2048 + 2048)], dma=True,
                       f=lambda e, h2=h2, buf=buf: e.dma_start(
                           out=K.hbrow[0:1, buf * 512:buf * 512 + 512],
                           in_=h2[127:128, 512:1024]))
                P.step("gpsimd", reads=[r_h2], dma=True,
                       writes=[K.r_rows(dst, tt * TILE, (tt + 1) * TILE, rowb)],
                       f=lambda e, tt=tt, h2=h2: e.dma_start(
                           out=dst.ap()[tt * TILE:(tt + 1) * TILE, :], in_=h2[:, :]))
                P.mark_split2()
                hs = K.hsx[buf]
                r_hs = ("arena", (K.A_HSX + buf * 2048) * 2,
                        (K.A_HSX + buf * 2048 + 2048) * 2)
                r_hs_a = ("arena", (K.A_HSX + buf * 2048) * 2,
                          (K.A_HSX + buf * 2048 + 1024) * 2)
                r_hs_b = ("arena", (K.A_HSX + buf * 2048 + 1024) * 2,
                          (K.A_HSX + buf * 2048 + 2048) * 2)
                P.step("vector", reads=[r_h2], writes=[r_hs_a],
                       f=lambda e, h2=h2, hs=hs: e.tensor_copy(hs[:, 0:512],
                                                               h2[:, 0:512]))
                P.step("sync", reads=[r_h2], writes=[r_hs_b],
                       dma=True,
                       f=lambda e, h2=h2, hs=hs: e.dma_start(
                           out=hs[1:128, 512:1024], in_=h2[0:127, 512:1024]))
                if tt == tiles[len(kv_tiles)]:
                    P.step("vector", writes=[r_hs_b],
                           f=lambda e, hs=hs: e.memset(hs[0:1, 512:1024], 0.0))
                else:
                    pv = (1 - buf) * 2048
                    P.step("sync",
                           reads=[("hbrow", pv, pv + 2048)],
                           writes=[r_hs_b], dma=True,
                           f=lambda e, hs=hs, buf=buf: e.dma_start(
                               out=hs[0:1, 512:1024],
                               in_=K.hbrow[0:1, (1 - buf) * 512:(1 - buf) * 512 + 512]))
                rcf = 56 + 2 * buf
                self._ln_stats([r_hs], hs[:, :], D, rcf, rcf + 1, par=8 + buf)
                P.step("scalar", reads=[r_hs, K.r_cols(K.stt, rcf, rcf + 2, 4)],
                       writes=[r_hs],
                       f=lambda e, hs=hs, rcf=rcf: e.activation(
                           hs[:, :], hs[:, :], AF.Identity,
                           scale=K.stt[:, rcf:rcf + 1], bias=K.stt[:, rcf + 1:rcf + 2]))
                P.step("vector", reads=[r_hs, r_lng], writes=[r_hs],
                       f=lambda e, hs=hs: e.tensor_tensor(hs[:, :], hs[:, :], lng,
                                                          OP.mult))
                hl = K.tbf[buf]
                r_hl = K.r_all(hl)
                P.step("vector", reads=[r_hs, r_lnb], writes=[r_hl],
                       f=lambda e, hs=hs, hl=hl: e.tensor_tensor(hl[:, :], hs[:, :],
                                                                 lnb, OP.add))
                P.step("gpsimd", reads=[r_hl], dma=True,
                       writes=[K.r_rows(K.d_hln, tt * TILE, (tt + 1) * TILE, D * 2)],
                       f=lambda e, tt=tt, hl=hl: e.dma_start(
                           out=K.d_hln.ap()[tt * TILE:(tt + 1) * TILE, :],
                           in_=hl[:, :]))
            first = False
            front, heads, fold = P.capture3()
            if l == 1:
                heads = heads + fold
                fold = []
            b = pend_fold2 + front
            if pend_heads and K.il_lag is not None:
                P.emit_interleaved(pend_heads, b, lag=K.il_lag)
            else:
                if pend_heads:
                    P.steps.extend(pend_heads)
                P.steps.extend(b)
            pend_heads = heads
            pend_fold2 = pend_fold1
            pend_fold1 = fold
        if pend_heads:
            if K.il_lag is not None:
                P.emit_interleaved(pend_heads, pend_fold2, lag=K.il_lag)
            else:
                P.steps.extend(pend_heads)
                P.steps.extend(pend_fold2)
            P.steps.extend(pend_fold1)
        else:
            P.steps.extend(pend_fold2)
            P.steps.extend(pend_fold1)
        P.step("gpsimd", writes=[K.r_ar(K.A_W1, 7 * H2),
                                 ("kfm_slot", 0, 6), ("vsp_slot", 0, 6)], dma=True,
               f=lambda e: e.dma_start(
                   out=K.ar(K.A_W1, 7 * H2).rearrange("p (c n) -> p c n", c=7),
                   in_=K.i_w1.ap()[l][0:7 * TILE].rearrange("(c p) n -> p c n",
                                                            p=TILE)))

    def ph_Bf(self, l):
        P, K = self.P, self
        rowb = D * 4
        rowb2 = D * 2
        src = K.d_sa
        dst = K.d_sb
        tiles = list(B0F if l == 0 else B1F)
        t_lo = tiles[0]
        P.phase(f"B{l}f_g")
        P.step("gpsimd", writes=[K.r_all(K.b1_bf)], dma=True,
               f=lambda e: e.dma_start(out=K.b1_bf[:, :], in_=K.i_b1.ap()[l]))
        P.step("gpsimd", writes=[K.r_all(K.b2_bf)], dma=True,
               f=lambda e: e.dma_start(out=K.b2_bf[:, :], in_=K.i_b2.ap()[l]))
        r_w1 = K.r_ar(K.A_W1, 8 * H2)
        r_w2 = K.r_ar(K.A_W2, 22 * 1024)
        w1w = lambda c, lo, n: K.ar(K.A_W1 + c * H2 + lo, n)
        w2w = lambda c, lo, n: K.ar(K.A_W2 + c * 1024 + lo, n)
        chunks = [(i * 512, min(512, HID - i * 512)) for i in range(cdiv(HID, 512))]
        for tt in tiles:
            buf = tt % 2
            row0 = tt * TILE
            hl = K.tbf[buf]
            r_hl = K.r_all(hl)
            P.step("sync", writes=[r_hl], dma=True,
                   f=lambda e, row0=row0, hl=hl: e.dma_start(
                       out=hl[:, :], in_=K.d_hln.ap()[row0:row0 + TILE, :]))
            xm = K.tin[buf]
            r_xm = K.r_all(xm)
            P.step("sync", writes=[r_xm], dma=True,
                   f=lambda e, row0=row0, xm=xm: e.dma_start(
                       out=xm[:, :], in_=src.ap()[row0:row0 + TILE, :]))
            mr = K.mrow[buf]
            r_mr = K.r_all(mr)
            P.step("sync", writes=[r_mr], dma=True,
                   f=lambda e, row0=row0, mr=mr: e.dma_start(
                       out=mr[:, :], in_=K.i_rmask.ap()[PRE + row0:PRE + row0 + TILE, :]))
            hfm = K.h1fm[buf]
            r_hfm = K.r_all(hfm)
            self._transpose8(hl, [r_hl], K.pt0, hfm[:, :], [r_hfm])
            r_pA0, r_pA1 = K.r_bank(K.psA, 0), K.r_bank(K.psA, 1)
            for ci, (c0, cn) in enumerate(chunks):
                if ci % 2 == 0:
                    pa, pg = K.pb0[:, :], K.pb1[:, :]
                    r_pa, r_pg = K.r_all(K.pb0), K.r_all(K.pb1)
                else:
                    pa, pg = K.psB[:, 0:512], K.psB[:, 512:1024]
                    r_pa, r_pg = K.r_bank(K.psB, 0), K.r_bank(K.psB, 1)
                gfmB = K.gfmB_[ci % 2]
                r_gfm = K.r_all(gfmB)
                def mma(e, c0=c0, cn=cn, hfm=hfm, pa=pa):
                    for c in range(8):
                        e.matmul(pa[:, 0:cn], hfm[:, c * TILE:(c + 1) * TILE],
                                 w1w(c, c0, cn), start=(c == 0), stop=False)
                    return e.matmul(pa[:, 0:cn], K.ones_row[:, :],
                                    K.b1_bf[:, c0:c0 + cn], start=False, stop=True)
                P.step("tensor", reads=[r_hfm, r_w1, K.r_all(K.b1_bf),
                                        K.r_all(K.ones_row)], writes=[r_pa], f=mma)
                def mmg(e, c0=c0, cn=cn, hfm=hfm, pg=pg):
                    for c in range(8):
                        e.matmul(pg[:, 0:cn], hfm[:, c * TILE:(c + 1) * TILE],
                                 w1w(c, HID + c0, cn), start=(c == 0), stop=False)
                    return e.matmul(pg[:, 0:cn], K.ones_row[:, :],
                                    K.b1_bf[:, HID + c0:HID + c0 + cn],
                                    start=False, stop=True)
                P.step("tensor", reads=[r_hfm, r_w1, K.r_all(K.b1_bf),
                                        K.r_all(K.ones_row)], writes=[r_pg], f=mmg)
                gelx = (K.gel, K.gel2)[ci % 2]
                glbf = (K.glbf, K.glbf2)[ci % 2]
                P.step("scalar", reads=[r_pg], writes=[K.r_all(gelx)],
                       f=lambda e, cn=cn, gelx=gelx, pg=pg: e.activation(
                           gelx[:, 0:cn], pg[:, 0:cn], AF.Gelu))
                P.step("vector", reads=[r_pa, K.r_all(gelx)], writes=[K.r_all(glbf)],
                       f=lambda e, cn=cn, gelx=gelx, pa=pa, glbf=glbf: e.tensor_tensor(
                           glbf[:, 0:cn], pa[:, 0:cn], gelx[:, 0:cn], OP.mult))
                if K.dbg and ci == 0:
                    P.step("gpsimd", reads=[K.r_all(K.glbf)], dma=True,
                           writes=[("dbgbuf", 1, 2)],
                           f=lambda e, tt=tt: e.dma_start(
                               out=K.d_dbg.ap()[tt * TILE:(tt + 1) * TILE, 0:512],
                               in_=K.glbf[:, :]))
                    P.step("gpsimd", reads=[K.r_all(K.gel)], dma=True,
                           writes=[("dbgbuf", 1, 2)],
                           f=lambda e, tt=tt: e.dma_start(
                               out=K.d_dbg.ap()[tt * TILE:(tt + 1) * TILE, 512:1536]
                               .bitcast(mybir.dt.float32),
                               in_=K.gel[:, :]))
                nch = cdiv(cn, TILE)
                ptx = (K.pt1, K.pt0)[ci % 2]
                def trg(e, cn=cn, nch=nch, glbf=glbf, ptx=ptx):
                    last = None
                    for c in range(nch):
                        w = min(TILE, cn - c * TILE)
                        last = e.transpose(ptx[0:w, c * TILE:(c + 1) * TILE],
                                           glbf[:, c * TILE:c * TILE + w],
                                           K.ident[:, :])
                    return last
                P.step("tensor", reads=[K.r_all(glbf), K.r_all(K.ident)],
                       writes=[K.r_all(ptx)], f=trg)
                P.step("vector", reads=[K.r_all(ptx)], writes=[r_gfm],
                       f=lambda e, nch=nch, gfmB=gfmB, ptx=ptx: e.tensor_copy(
                           gfmB[:, 0:nch * TILE], ptx[:, 0:nch * TILE]))
                def mw2b(e, ci=ci, c0=c0, cn=cn, nch=nch, gfmB=gfmB):
                    last = None
                    for c in range(nch):
                        w = min(TILE, cn - c * TILE)
                        cc = (c0 + c * TILE) // TILE
                        for g in range(2):
                            last = e.matmul(
                                K.psA[:, g * 512:(g + 1) * 512],
                                gfmB[0:w, c * TILE:(c + 1) * TILE],
                                K.ar(K.A_W2 + cc * 1024 + g * 512, 512)[0:w, :],
                                start=(ci == 0 and c == 0), stop=False)
                    return last
                P.step("tensor", reads=[r_gfm, r_w2],
                       writes=[r_pA0, r_pA1], f=mw2b)
            P.step("tensor", reads=[K.r_all(K.ones_row), K.r_all(K.b2_bf)],
                   writes=[r_pA0, r_pA1],
                   f=lambda e: (e.matmul(K.psA[:, 0:512], K.ones_row[:, :],
                                         K.b2_bf[:, 0:512], start=False, stop=True),
                                e.matmul(K.psA[:, 512:1024], K.ones_row[:, :],
                                         K.b2_bf[:, 512:1024], start=False, stop=True))[1])
            if K.dbg:
                P.step("vector", reads=[r_pA0], writes=[K.r_all(K.gel)],
                       f=lambda e: e.tensor_copy(K.gel[:, :], K.psA[:, 0:512]))
                P.step("gpsimd", reads=[K.r_all(K.gel)], dma=True,
                       writes=[("dbgbuf", 1, 2)],
                       f=lambda e, tt=tt: e.dma_start(
                           out=K.d_dbg.ap()[tt * TILE:(tt + 1) * TILE, 1536:2560]
                           .bitcast(mybir.dt.float32),
                           in_=K.gel[:, :]))
                P.step("gpsimd", reads=[K.r_all(K.gfmB_[1])], dma=True,
                       writes=[("dbgbuf", 1, 2)],
                       f=lambda e, tt=tt: e.dma_start(
                           out=K.d_dbg.ap()[tt * TILE:(tt + 1) * TILE, 2560:3072],
                           in_=K.gfmB_[1][:, :]))
            P.step("vector", reads=[r_pA0, r_pA1, r_xm, r_mr], writes=[r_xm],
                   f=lambda e, xm=xm, mr=mr: e.scalar_tensor_tensor(
                       out=xm[:, :], in0=K.psA[:, :], scalar=mr[:, 0:1],
                       in1=xm[:, :], op0=OP.mult, op1=OP.add))
            P.step("gpsimd", reads=[r_xm], dma=True,
                   writes=[K.r_rows(dst, tt * TILE, (tt + 1) * TILE, rowb)],
                   f=lambda e, tt=tt, xm=xm: e.dma_start(
                       out=dst.ap()[tt * TILE:(tt + 1) * TILE, :], in_=xm[:, :]))

    def ph_C(self):
        P, K = self.P, self
        P.phase("C")
        rowb = D * 4
        for i, tt in enumerate(CT):
            buf = tt % 2
            row0 = tt * TILE
            ft = K.tin[buf]
            r_ft = K.r_all(ft)
            P.step("sync", writes=[r_ft], dma=True,
                   f=lambda e, row0=row0, ft=ft: e.dma_start(
                       out=ft[:, :], in_=K.d_sb.ap()[row0:row0 + TILE, :]))
            x1 = K.taux[buf]
            r_x1 = K.r_all(x1)
            P.step("sync", writes=[r_x1], dma=True,
                   f=lambda e, row0=row0, x1=x1: e.dma_start(
                       out=x1[:, :], in_=K.d_x1.ap()[row0:row0 + TILE, :]))
            rc = buf * 2
            self._ln_stats([r_ft], ft[:, :], D, rc, rc + 1, par=buf)
            lnf = K.tbf[buf]
            lo = K.qc[0] if buf == 0 else K.qc[1]
            ob = K.taux[1 - buf]
            r_ob = K.r_all(ob)
            P.step("scalar", reads=[r_ft, K.r_cols(K.stt, rc, rc + 2, 4)],
                   writes=[r_ob],
                   f=lambda e, ft=ft, ob=ob, rc=rc: e.activation(
                       ob[:, :], ft[:, :], AF.Identity,
                       scale=K.stt[:, rc:rc + 1], bias=K.stt[:, rc + 1:rc + 2]))
            P.step("vector", reads=[r_ob, r_x1], writes=[r_ob],
                   f=lambda e, ob=ob, x1=x1: e.tensor_tensor(
                       ob[:, :], ob[:, :], x1[:, :], OP.add))
            P.step("gpsimd", reads=[r_ob], dma=True,
                   writes=[K.r_rows(K.o_y, i * TILE, (i + 1) * TILE, rowb)],
                   f=lambda e, i=i, ob=ob: e.dma_start(
                       out=K.o_y.ap()[i * TILE:(i + 1) * TILE, :], in_=ob[:, :]))
        P.step("gpsimd", reads=[K.r_rows(K.o_y, 0, OWN_T * TILE, rowb)],
               f=lambda e: e.nop())


def _rotary_tables():
    t = np.arange(8192, dtype=np.float64)
    half = np.arange(0, DH, 2, dtype=np.float64)
    inv_freq = 1.0 / 10000.0 ** (half / DH)
    freqs = t[:, None] * inv_freq[None, :]
    freqs = np.concatenate([freqs, freqs], -1)
    base = (half + 0.4 * DH) / (1.4 * DH)
    power = (t - 8192 // 2) / float(WIN // 2)
    scale = base[None, :] ** power[:, None]
    scale = np.concatenate([scale, scale], -1)
    cos, sin = np.cos(freqs), np.sin(freqs)
    return (np.float32(cos * scale), np.float32(sin * scale),
            np.float32(cos / scale), np.float32(sin / scale))


def _m01_masks(pad_rows):
    ii = np.arange(TILE)[:, None]
    jj = np.arange(640)[None, :]
    band = (jj >= ii) & (jj <= ii + WIN)
    m = np.zeros((SPAN_T, TILE, 640), dtype=np.float32)
    for tt in range(SPAN_T):
        kpos = tt * TILE - WIN + jj
        valid = kpos >= (pad_rows - PRE)
        m[tt] = (band & valid).astype(np.float32)
    return m.astype(BF)


def stage_inputs(L0_attn_x, L0_attn_qkv_w, L0_attn_qkv_b, L0_attn_out_w,
                 L0_attn_out_b, L0_ffn_wqkv, L0_ffn_q_scale, L0_ffn_k_scale,
                 L0_ffn_wo, L0_ffn_ln_g, L0_ffn_ln_b, L0_ffn_w1, L0_ffn_b1,
                 L0_ffn_w2, L0_ffn_b2):
    x0 = np.ascontiguousarray(np.asarray(L0_attn_x, dtype=np.float32))
    csq, ssq, csk, ssk = _rotary_tables()
    rot_full = np.concatenate([csq, ssq, csk, ssk], axis=1)

    rep = lambda v: np.ascontiguousarray(np.broadcast_to(
        np.asarray(v, np.float32)[None, :], (TILE, len(np.asarray(v)))))

    shared = {
        "qkvwT": np.ascontiguousarray(np.asarray(L0_attn_qkv_w, np.float32).T),
        "qkvb": np.asarray(L0_attn_qkv_b, np.float32).reshape(1, -1),
        "outwT": np.ascontiguousarray(np.asarray(L0_attn_out_w, np.float32).T),
        "outb": np.asarray(L0_attn_out_b, np.float32).reshape(1, -1),
        "wqkv": np.ascontiguousarray(np.asarray(L0_ffn_wqkv, np.float32)),
        "wo": np.ascontiguousarray(np.asarray(L0_ffn_wo, np.float32)),
        "w1": np.ascontiguousarray(np.asarray(L0_ffn_w1, np.float32)),
        "b1": np.ascontiguousarray(np.asarray(L0_ffn_b1, np.float32).reshape(2, 1, -1)),
        "w2": np.ascontiguousarray(np.asarray(L0_ffn_w2, np.float32)),
        "b2": np.ascontiguousarray(np.asarray(L0_ffn_b2, np.float32).reshape(2, 1, -1)),
        "cmaskI": (np.arange(TILE)[:, None] <= np.arange(TILE)[None, :]).astype(BF),
        "identI": np.eye(TILE, dtype=BF),
        "epsc": np.broadcast_to(
            np.array([LNEPS, 0, 0, 0, 0, 0, 0, 0], np.float32)[None, :],
            (TILE, 8)).copy(),
    }
    scl = np.zeros((2, TILE, 256), np.float32)
    lnrep = np.zeros((2, TILE, 2048), np.float32)
    qs = np.asarray(L0_ffn_q_scale, np.float32)
    ks = np.asarray(L0_ffn_k_scale, np.float32)
    g_ = np.asarray(L0_ffn_ln_g, np.float32)
    b_ = np.asarray(L0_ffn_ln_b, np.float32)
    for l in (0, 1):
        scl[l, :, 0:64] = rep(qs[l])
        scl[l, :, 64:128] = rep(np.roll(qs[l], 32))
        scl[l, :, 128:192] = rep(ks[l])
        scl[l, :, 192:256] = rep(np.roll(ks[l], 32))
        lnrep[l, :, 0:1024] = rep(g_[l])
        lnrep[l, :, 1024:2048] = rep(b_[l])
    shared["scl"] = scl
    shared["lnrep"] = lnrep

    in_maps = []
    for c in range(8):
        s, q = c // 4, c % 4
        t0 = q * 2048
        pad = 8192 - (t0 + 2048)
        xs = np.zeros((8192, D), np.float32)
        if pad < 8192:
            xs[pad:] = x0[s, 0:t0 + 2048]
        rm = np.zeros((8192, 1), np.float32)
        rm[pad:] = 1.0
        absr = np.clip(np.arange(SPAN) + t0 - HALO_T * TILE, 0, 8191)
        m = {k: v for k, v in shared.items()}
        m["x0stage"] = xs
        m["rmask"] = rm
        m["rot"] = np.ascontiguousarray(rot_full[absr])
        m["m01"] = _m01_masks(pad)
        in_maps.append(m)
    return in_maps


def kernel(**inputs):
    in_maps = stage_inputs(**inputs)
    with ExitStack() as es:
        kb = KB(es)
        nc = kb.build()
        res = run_bass_kernel_spmd(nc, in_maps, core_ids=list(range(8)))
    out = np.empty((2, 8192, D), np.float32)
    for c in range(8):
        s, q = c // 4, c % 4
        out[s, q * 2048:(q + 1) * 2048] = res.results[c]["y"]
    return out
